# version 10
# baseline (speedup 1.0000x reference)
import contextlib
import numpy as np
import ml_dtypes
import concourse.bass as bass
import concourse.mybir as mybir
from concourse.bass_utils import run_bass_kernel_spmd

F32 = mybir.dt.float32
BF16 = mybir.dt.bfloat16
AF = mybir.ActivationFunctionType
ALU = mybir.AluOpType

DBG = False
NCORES = 8
D = 2048
DT = 16
FF = 8192
FT = 64
S2048 = float(1.0 / (np.sqrt(2048.0) * 64.0))
S8192 = float(1.0 / (np.sqrt(8192.0) * 64.0))

VC_TMK, VC_TMV, VC_TMR = 0, 16, 32
VC_W, VC_U, VC_BK, VC_BV, VC_BR, VC_BO = 48, 64, 80, 96, 112, 128
VC_SUG, VC_WDP = 144, 160
VC_WSU, VC_SUDP, VC_GSC = 176, 240, 304
VC_PM, VC_PMI = 368, 376
VC_ONES = 384
AX_BKU, AX_WDP64, AX_PDEC, AX_ACAR, AX_BCAR = 0, 16, 32, 48, 64
AX_A0, AX_B0, AX_UPS, AX_GSC64, AX_RTMP, AX_FT, AX_TT = 80, 96, 112, 176, 240, 256, 272
AX_ACAR2, AX_BCAR2, AX_A02 = 288, 304, 320
HM_H128, HM_I8H16, HM_I2H64, HM_ONE = 0, 128, 256, 384


def build_program(Tc, C):
    NCH = Tc // C
    TH = Tc // 512
    W1 = Tc + 1
    assert Tc % 512 == 0 and C * NCH == Tc and NCH >= 2

    nc = bass.Bass("TRN2", target_bir_lowering=False, detect_race_conditions=False)

    xin = nc.dram_tensor("xin", [D, W1], F32, kind="ExternalInput").ap()
    wk = nc.dram_tensor("wk", [DT, 128, DT * 128], F32, kind="ExternalInput").ap()
    wv = nc.dram_tensor("wv", [DT, 128, DT * 128], F32, kind="ExternalInput").ap()
    wr = nc.dram_tensor("wr", [DT, 128, DT * 128], F32, kind="ExternalInput").ap()
    wo = nc.dram_tensor("wo", [DT, 128, DT * 128], F32, kind="ExternalInput").ap()
    wup = nc.dram_tensor("wup", [FT, 128, 16 * 128], F32, kind="ExternalInput").ap()
    wgt = nc.dram_tensor("wgt", [FT, 128, 16 * 128], F32, kind="ExternalInput").ap()
    wdn = nc.dram_tensor("wdn", [DT * 2, 128, 32 * 128], F32, kind="ExternalInput").ap()
    vecs = nc.dram_tensor("vecs", [128, 512], F32, kind="ExternalInput").ap()
    hmat = nc.dram_tensor("hmat", [128, 385], BF16, kind="ExternalInput").ap()
    out = nc.dram_tensor("out", [128, DT * Tc], F32, kind="ExternalOutput").ap()

    e1sp = nc.dram_tensor("e1sp", [DT, 128, Tc], F32, kind="Internal").ap()
    asp = nc.dram_tensor("asp", [DT, 128, Tc], BF16, kind="Internal").ap()
    e2sp = nc.dram_tensor("e2sp", [DT, 128, Tc], BF16, kind="Internal").ap()
    rinsp = nc.dram_tensor("rinsp", [128, DT * Tc], BF16, kind="Internal").ap()
    yb2k = nc.dram_tensor("yb2k", [128, DT, Tc], BF16, kind="Internal").ap()
    yb8k = nc.dram_tensor("yb8k", [128, FT, Tc], BF16,
                          kind=("ExternalOutput" if DBG else "Internal")).ap()
    if DBG:
        dbg_xn = nc.dram_tensor("dbg_xn", [128, DT * W1], BF16, kind="ExternalOutput").ap()
        dbg_kk = nc.dram_tensor("dbg_kk", [128, DT * Tc], BF16, kind="ExternalOutput").ap()
        dbg_v = nc.dram_tensor("dbg_v", [128, DT * Tc], BF16, kind="ExternalOutput").ap()
        dbg_kin = nc.dram_tensor("dbg_kin", [128, DT * Tc], BF16, kind="ExternalOutput").ap()
        dbg_x1 = nc.dram_tensor("dbg_x1", [128, DT * Tc], BF16, kind="ExternalOutput").ap()
        dbg_wkv = nc.dram_tensor("dbg_wkv", [128, DT * Tc], BF16, kind="ExternalOutput").ap()
        dbg_aux = nc.dram_tensor("dbg_aux", [128, 352], F32, kind="ExternalOutput").ap()
        dbg_wkvr = nc.dram_tensor("dbg_wkvr", [128, DT * Tc], BF16, kind="ExternalOutput").ap()
        dbg_g = nc.dram_tensor("dbg_g", [128, DT * Tc], BF16, kind="ExternalOutput").ap()
        dbg_gath = nc.dram_tensor("dbg_gath", [128, NCORES * 48], F32, kind="ExternalOutput").ap()
    cc_in = nc.dram_tensor("cc_in", [128, 48], F32, kind="Internal").ap()
    cc_out = nc.dram_tensor("cc_out", [NCORES * 128, 48], F32, kind="Internal").ap()

    ctx = contextlib.ExitStack()
    E = ctx.enter_context
    with ctx:
        OF_AR1 = 16512
        OF_AR2 = OF_AR1 + 32800
        OF_ARZ = OF_AR2 + 32768
        OF_ARS = OF_ARZ + 65536
        OF_RING = OF_ARS + 32768
        OF_UG = OF_RING + 24576
        OF_VEC = OF_UG + 12288
        OF_AUX = OF_VEC + 2048
        OF_HM = OF_AUX + 1408
        OF_GATH = OF_HM + 800
        AT = nc.alloc_sbuf_tensor_at
        AR1 = AT("AR1", [128, DT * W1], BF16, offset=OF_AR1)
        AR2 = AT("AR2", [128, DT * Tc], BF16, offset=OF_AR2)
        ARZ = AT("ARZ", [128, 2 * DT * Tc], BF16, offset=OF_ARZ)
        ARS = AT("ARS", [128, 8192], F32, offset=OF_ARS)
        RING = AT("RING", [128, 6 * 2048], BF16, offset=OF_RING)
        B_ug = AT("B_ug", [128, 6 * 1024], BF16, offset=OF_UG)
        B_vec = AT("B_vec", [128, 512], F32, offset=OF_VEC)
        B_aux = AT("B_aux", [128, 352], F32, offset=OF_AUX)
        B_hm = AT("B_hm", [128, 385], BF16, offset=OF_HM)
        B_gath = AT("B_gath", [128, NCORES * 48], F32, offset=OF_GATH)
        B_slk = AT("B_slk", [128, 256], F32, offset=OF_GATH + 1536)
        VQ = AT("VQ", [128, 4 * 1024], BF16, offset=OF_AR2)
        VRIN = AT("VRIN", [128, DT * Tc], BF16, offset=OF_ARS)
        VEV = AT("VEV", [128, 1024], F32, offset=OF_UG)

        def xsl(tau, a, b):
            return AR1[:, tau * W1 + a: tau * W1 + b]

        def x1sl(a, b):
            return AR1[:, a:b]

        def gsl(tau, a, b):
            return AR1[:, tau * Tc + a: tau * Tc + b]

        kin = AR2
        wkvr = AR2
        kkv = ARZ
        KKOFF, VOFF = 0, DT * Tc

        def kksl(tau, a, b):
            return ARZ[:, KKOFF + tau * Tc + a: KKOFF + tau * Tc + b]

        def vsl(tau, a, b):
            return ARZ[:, VOFF + tau * Tc + a: VOFF + tau * Tc + b]

        def wkvsl(tau, a, b):
            return ARZ[:, tau * Tc + a: tau * Tc + b]

        def z3sl(a, b):
            return ARZ[:, a:b]

        SC_NT, SC_MX, SC_RS, SC_M, SC_R = 0, 1025, 3073, 5121, 6146
        SC_TB, SC_L, SC_E1 = 0, DT * C, 2 * DT * C
        SC_OT, SC_XF = 0, 2048

        def scr(c0, n):
            return ARS[:, c0:c0 + n]

        QE2, QA = 0, DT * C

        def e2sl(tau, a=0, b=None):
            b = C if b is None else b
            return VQ[:, QE2 + tau * C + a: QE2 + tau * C + b]

        def asl(tau, a=0, b=None):
            b = C if b is None else b
            return VQ[:, QA + tau * C + a: QA + tau * C + b]

        def vc(c0, n=16):
            return B_vec[:, c0:c0 + n]

        def vcol(c0):
            return B_vec[:, c0:c0 + 1]

        def acol(c0):
            return B_aux[:, c0:c0 + 1]

        B_mrow = ARS[0:1, 1025:1025 + W1]
        B_qrow = ARS[0:1, 1025 + W1:1025 + 2 * W1]
        ones_row = B_vec[0:1, VC_ONES:VC_ONES + 128]
        ones_col = B_hm[:, HM_ONE:HM_ONE + 1]

        ps = [E(nc.psum_tensor(f"ps{i}", [128, 512], F32)) for i in range(4)]
        psF = [E(nc.psum_tensor(f"psF{i}", [128, 512], F32)) for i in range(2)]
        psS = E(nc.psum_tensor("psS", [1, 512], F32))
        psQ = E(nc.psum_tensor("psQ", [1, 512], F32))

        s_xld = E(nc.semaphore("s_xld"))
        s_wsl = [E(nc.semaphore(f"s_wsl{i}")) for i in range(6)]
        s_ld = E(nc.semaphore("s_ld"))
        s_rsp = [E(nc.semaphore(f"s_rsp{i}")) for i in range(2)]
        s_sh = E(nc.semaphore("s_sh"))
        s_spe1 = E(nc.semaphore("s_spe1"))
        s_spa = E(nc.semaphore("s_spa"))
        s_spe2 = E(nc.semaphore("s_spe2"))
        s_rl = E(nc.semaphore("s_rl"))
        s_rlb = E(nc.semaphore("s_rlb"))
        s_put = E(nc.semaphore("s_put"))
        s_gt = E(nc.semaphore("s_gt"))
        s_ri = E(nc.semaphore("s_ri"))
        s_zw = E(nc.semaphore("s_zw"))
        s_xf = [E(nc.semaphore(f"s_xf{i}")) for i in range(2)]
        s_st = [E(nc.semaphore(f"s_st{i}")) for i in range(2)]
        s_za = [E(nc.semaphore(f"s_za{i}")) for i in range(2)]
        s_cba = E(nc.semaphore("s_cba"))
        s_cbb = E(nc.semaphore("s_cbb"))
        s_cbc = E(nc.semaphore("s_cbc"))
        s_cc = E(nc.semaphore("s_cc"))
        s_pe = E(nc.semaphore("s_pe"))
        s_ev = E(nc.semaphore("s_ev"))
        s_vp = E(nc.semaphore("s_vp"))
        s_sc = E(nc.semaphore("s_sc"))
        s_gp = E(nc.semaphore("s_gp"))

        plan = {"sync": [], "gpsimd": [], "vector": [], "scalar": [], "tensor": []}
        cnt = {}

        def op(eng, action, waits=(), inc=None, amt=1):
            ev = None
            if inc is not None:
                cnt[inc] = cnt.get(inc, 0) + (amt if amt else 16)
                ev = (inc, cnt[inc])
            plan[eng].append((tuple(w for w in waits if w is not None), action, inc, amt))
            return ev

        def last(sem):
            return (sem, cnt.get(sem, 0))

        def vslack(waits, n=2):
            ev = None
            for i in range(n):
                ev = op("vector", lambda v: v.tensor_scalar_mul(
                    B_slk[:, 0:256], B_slk[:, 0:256], 1.0),
                    waits=(list(waits) if i == 0 else []), inc=s_vp)
            return ev

        xin_r = xin.rearrange("(tau p) t -> p tau t", p=128)
        ev_xl = []
        for i in range(4):
            ev_xl.append(op("gpsimd", lambda g, i=i: g.dma_start(
                out=AR1[:, i * 4 * W1:(i + 1) * 4 * W1].rearrange("p (a t) -> p a t", t=W1),
                in_=xin_r[:, i * 4:(i + 1) * 4, :]), inc=s_xld, amt=16))

        NSLOT = 6
        wstate = {"cursor": 0, "used": 0, "grab": 0}
        wblocks = []
        wreleases = []
        wdefer = {}

        def wload(src, nsl):
            cur = wstate["cursor"]
            assert cur + nsl <= NSLOT, "ring wrap-skip"
            waits = []
            need = wstate["used"] + nsl - NSLOT
            if need > 0:
                marker = ("DEFER_REL", len(wdefer))
                wdefer[marker] = need
                waits.append(marker)
            wstate["used"] += nsl
            wstate["cursor"] = (cur + nsl) % NSLOT
            ev = op("gpsimd", lambda g, cur=cur, nsl=nsl, src=src: g.dma_start(
                out=RING[:, cur * 2048:(cur + nsl) * 2048], in_=src),
                waits=waits, inc=s_wsl[cur], amt=16)
            wblocks.append((cur, nsl, ev))
            return ev

        def wgrab():
            bi = wstate["grab"]
            wstate["grab"] += 1
            sl, nsl, ev = wblocks[bi]
            return RING[:, sl * 2048:(sl + nsl) * 2048], ev, nsl

        def wrelease(nsl, eng=None, waits=()):
            prev = wreleases[-1][0] if wreleases else 0
            wreleases.append((prev + nsl, last(s_pe)))

        for t_ in range(DT):
            wload(wk[t_], 1)
        for t_ in range(DT):
            wload(wv[t_], 1)

        op("sync", lambda s: s.dma_start(out=B_vec[:], in_=vecs), inc=s_ld, amt=16)
        op("sync", lambda s: s.dma_start(out=B_hm[:], in_=hmat), inc=s_ld, amt=16)
        ev_vec = (s_ld, 32)
        ev_hm = (s_ld, 32)

        op("vector", lambda v: v.memset(B_slk[:], 0.0))
        op("vector", lambda v: v.tensor_add(B_aux[:, AX_BKU:AX_BKU + 16],
                                            vc(VC_U), vc(VC_BK)), waits=[ev_vec])
        op("vector", lambda v: v.tensor_scalar_mul(B_aux[:, AX_WDP64:AX_WDP64 + 16],
                                                   vc(VC_WDP), 64.0))
        ev_u1 = op("vector", lambda v: v.tensor_mul(B_aux[:, AX_UPS:AX_UPS + 64],
                                                    vc(VC_WSU, 64), vc(VC_SUDP, 64)),
                   inc=s_vp)
        ev_u2 = vslack([ev_u1])
        op("vector", lambda v: v.tensor_scalar_mul(B_aux[:, AX_UPS:AX_UPS + 64],
                                                   B_aux[:, AX_UPS:AX_UPS + 64], 64.0),
           waits=[ev_u2])
        op("vector", lambda v: v.tensor_scalar_mul(B_aux[:, AX_GSC64:AX_GSC64 + 64],
                                                   vc(VC_GSC, 64), 64.0))
        op("vector", lambda v: v.memset(B_aux[:, AX_PDEC:AX_PDEC + 16], 0.0))
        op("vector", lambda v: v.memset(B_aux[:, AX_ACAR2:AX_ACAR2 + 32], 0.0))
        ev_aux0 = op("vector", lambda v: v.memset(B_aux[:, AX_A0:AX_A0 + 32], 0.0),
                     inc=s_vp)

        chunks0 = [(i * 512, 512) for i in range(Tc // 512)] + [(Tc, 1)]
        ev_sqmm = {}
        ev_rows = []
        for ci, (c0, n) in enumerate(chunks0):
            for tau in range(DT):
                waits = []
                if ci == 0:
                    waits.append((s_xld, 64))
                    if tau == 0:
                        waits.append(ev_hm)
                if ci > 0 and tau == 0:
                    waits.append(ev_rows[ci - 1])
                op("tensor", lambda t, tau=tau, c0=c0, n=n: t.matmul(
                    psS[0:1, 0:n], ones_col, xsl(tau, c0, c0 + n),
                    start=(tau == 0), stop=(tau == DT - 1)),
                   waits=waits, inc=(s_pe if tau == DT - 1 else None))
            ev_ssum = last(s_pe)
            for tau in range(DT):
                sqv = B_ug[:, (tau % 2) * 1024:(tau % 2) * 1024 + n]
                waits = [(s_xld, 64)] if ci == 0 else []
                if (ci, tau - 2) in ev_sqmm:
                    waits.append(ev_sqmm[(ci, tau - 2)])
                elif ci > 0 and tau < 2:
                    waits.append(ev_sqmm[(ci - 1, DT - 2 + tau)])
                if ci > 0 and tau == 0:
                    waits.append(ev_rows[ci - 1])
                evq = op("scalar", lambda s, tau=tau, c0=c0, n=n, sqv=sqv: s.activation(
                    sqv, xsl(tau, c0, c0 + n), AF.Square), waits=waits, inc=s_sc)
                ev_sqmm[(ci, tau)] = op("tensor", lambda t, tau=tau, n=n, sqv=sqv: t.matmul(
                    psQ[0:1, 0:n], ones_col, sqv,
                    start=(tau == 0), stop=(tau == DT - 1)),
                    waits=[evq], inc=s_pe)
            ev_qsum = last(s_pe)
            op("vector", lambda v, c0=c0, n=n: v.tensor_copy(
                B_mrow[0:1, c0:c0 + n], psS[0:1, 0:n]), waits=[ev_ssum])
            ev_rows.append(op("vector", lambda v, c0=c0, n=n: v.tensor_copy(
                B_qrow[0:1, c0:c0 + n], psQ[0:1, 0:n]), waits=[ev_qsum], inc=s_vp))

        op("vector", lambda v: v.tensor_scalar_mul(B_mrow[0:1, :], B_mrow[0:1, :], 1.0 / D))
        op("vector", lambda v: v.tensor_scalar_mul(B_qrow[0:1, :], B_qrow[0:1, :], 1.0 / D))
        op("vector", lambda v: v.tensor_mul(scr(SC_NT, W1)[0:1, :], B_mrow[0:1, :], B_mrow[0:1, :]))
        ev_var = op("vector", lambda v: v.tensor_sub(B_qrow[0:1, :], B_qrow[0:1, :],
                                                     scr(SC_NT, W1)[0:1, :]), inc=s_vp)
        ev_eps = op("vector", lambda v: v.tensor_scalar_add(
            B_qrow[0:1, :], B_qrow[0:1, :], 1e-5), inc=s_vp)
        op("scalar", lambda s: s.activation(B_qrow[0:1, :], B_qrow[0:1, :], AF.Ln),
           waits=[ev_eps], inc=s_sc)
        ev_rstd = op("scalar", lambda s: s.activation(B_qrow[0:1, :], B_qrow[0:1, :],
                                                      AF.Exp, scale=-0.5), inc=s_sc)

        ev_bc_last = None
        ev_bcR = []
        for ci, (c0, n) in enumerate(chunks0):
            pf = psF[ci % 2]
            waits = [ev_rstd, ev_var]
            if ci >= 2:
                waits.append(ev_bcR[ci - 2])
            e1_ = op("tensor", lambda t, c0=c0, n=n, pf=pf: t.matmul(
                pf[:, 0:n], ones_row, B_mrow[0:1, c0:c0 + n], start=True, stop=True),
                waits=waits, inc=s_pe)
            ec1 = op("vector", lambda v, c0=c0, n=n, pf=pf: v.tensor_copy(
                scr(SC_M, W1)[:, c0:c0 + n], pf[:, 0:n]), waits=[e1_], inc=s_vp)
            e2_ = op("tensor", lambda t, c0=c0, n=n, pf=pf: t.matmul(
                pf[:, 0:n], ones_row, B_qrow[0:1, c0:c0 + n], start=True, stop=True),
                waits=[ec1], inc=s_pe)
            ev_bc_last = op("vector", lambda v, c0=c0, n=n, pf=pf: v.tensor_copy(
                scr(SC_R, W1)[:, c0:c0 + n], pf[:, 0:n]), waits=[e2_], inc=s_vp)
            ev_bcR.append(ev_bc_last)

        ev_xn = []
        for tau in range(DT):
            nt = scr(SC_NT, W1)
            op("vector", lambda v, tau=tau, nt=nt: v.tensor_sub(
                nt, xsl(tau, 0, W1), scr(SC_M, W1)),
               waits=[ev_bc_last] if tau == 0 else [])
            ev_xn.append(op("vector", lambda v, tau=tau, nt=nt: v.tensor_mul(
                xsl(tau, 0, W1), nt, scr(SC_R, W1)), inc=s_vp))

        s_dbg = E(nc.semaphore("s_dbg")) if DBG else None
        ev_dbg_xn = None
        if DBG:
            ev_dbg_xn = op("sync", lambda s: s.dma_start(out=dbg_xn, in_=AR1[:]),
                           waits=[ev_xn[-1]], inc=s_dbg, amt=16)

        def mix(tm_c0, dst_flat_off, tau, waits=()):
            op("vector", lambda v, tau=tau: v.tensor_sub(
                scr(SC_MX, Tc), xsl(tau, 1, W1), xsl(tau, 0, W1 - 1)), waits=list(waits))
            return op("vector", lambda v, tau=tau, tm_c0=tm_c0, dst_flat_off=dst_flat_off:
                      v.scalar_tensor_tensor(
                          AR2[:, dst_flat_off + tau * Tc: dst_flat_off + (tau + 1) * Tc],
                          scr(SC_MX, Tc), vcol(tm_c0 + tau), xsl(tau, 0, W1 - 1),
                          ALU.mult, ALU.add), inc=s_vp)

        for tau in range(DT):
            mix(VC_TMK, 0, tau)
        ev_kmix = last(s_vp)

        def emit_proj(dst_kind, bias_base, rhs_off, first_waits, dst_fn):
            evs = []
            for e in range(DT):
                wview, wevt, nsl = wgrab()
                for th in range(TH):
                    gidx = e * TH + th
                    pbank = ps[gidx % 4]
                    for dtl in range(DT):
                        waits = []
                        if dtl == 0:
                            waits.append(wevt)
                            waits.append((s_ev, cnt.get(s_ev, 0) - 3))
                            if gidx == 0:
                                waits += list(first_waits)
                        op("tensor", lambda t, th=th, dtl=dtl, pbank=pbank, wview=wview,
                           rhs_off=rhs_off: t.matmul(
                               pbank[:, 0:512], wview[:, dtl * 128:(dtl + 1) * 128],
                               AR2[:, rhs_off + dtl * Tc + th * 512:
                                   rhs_off + dtl * Tc + th * 512 + 512]
                               if dst_kind != "r" else
                               VRIN[:, dtl * Tc + th * 512: dtl * Tc + th * 512 + 512],
                               start=(dtl == 0), stop=(dtl == DT - 1)),
                           waits=waits, inc=(s_pe if dtl == DT - 1 else None))
                    ev_grp = last(s_pe)
                    dst_ap, extra = dst_fn(e, th)
                    bias = acol(AX_BKU + e) if bias_base == "bku" else vcol(bias_base + e)
                    evs.append(op("scalar", lambda s, dst_ap=dst_ap, pbank=pbank, bias=bias:
                                  s.activation(dst_ap, pbank[:, 0:512], AF.Identity,
                                               bias=bias),
                                  waits=[ev_grp] + extra, inc=s_ev))
                wrelease(nsl)
            return evs

        ev_dbg_kin = None
        if DBG:
            ev_dbg_kin = op("sync", lambda s: s.dma_start(out=dbg_kin, in_=AR2[:]),
                            waits=[ev_kmix], inc=s_dbg, amt=16)
        ev_kk = emit_proj("kk", "bku", 0, [ev_kmix, ev_aux0] + ([ev_dbg_kin] if DBG else []),
                          lambda e, th: (kksl(e, th * 512, th * 512 + 512), []))

        if DBG:
            op("sync", lambda s: s.dma_start(out=dbg_kk, in_=ARZ[:, 0:DT * Tc]),
               waits=[ev_kk[-1]], inc=s_dbg, amt=16)
        for tau in range(DT):
            mix(VC_TMV, 0, tau, waits=[ev_kk[-1]] if tau == 0 else [])
        ev_vmix = last(s_vp)
        ev_vv = emit_proj("v", VC_BV, 0, [ev_vmix],
                          lambda e, th: (vsl(e, th * 512, th * 512 + 512), []))

        if DBG:
            op("sync", lambda s: s.dma_start(out=dbg_v, in_=ARZ[:, DT * Tc:2 * DT * Tc]),
               waits=[ev_vv[-1]], inc=s_dbg, amt=16)
        ev_rsp = []
        ev_rsl = []
        for tau in range(DT):
            rs = scr(SC_RS + (tau % 2) * 1024, Tc)
            op("vector", lambda v, tau=tau: v.tensor_sub(
                scr(SC_MX, Tc), xsl(tau, 1, W1), xsl(tau, 0, W1 - 1)))
            waits = [ev_rsl[tau - 2]] if tau >= 2 else []
            evm = op("vector", lambda v, tau=tau, rs=rs: v.scalar_tensor_tensor(
                rs, scr(SC_MX, Tc), vcol(VC_TMR + tau), xsl(tau, 0, W1 - 1),
                ALU.mult, ALU.add), waits=waits, inc=s_vp)
            stg = B_ug[:, (tau % 2) * 1024:(tau % 2) * 1024 + Tc]
            waits = [evm]
            if tau == 0:
                waits.append(ev_sqmm[(len(chunks0) - 1, DT - 1)])
            if tau >= 2:
                waits.append(ev_rsp[tau - 2])
            evsl = op("scalar", lambda s, stg=stg, rs=rs: s.activation(
                stg, rs, AF.Silu), waits=waits, inc=s_sc)
            ev_rsl.append(evsl)
            ev_rsp.append(op("sync", lambda s, tau=tau, stg=stg: s.dma_start(
                out=rinsp[:, tau * Tc:(tau + 1) * Tc], in_=stg),
                waits=[evsl], inc=s_rsp[tau % 2], amt=16))

        ev_xt = []
        for tau in range(DT):
            ev_xt.append(op("vector", lambda v, tau=tau: v.tensor_scalar_mul(
                xsl(tau, 1, W1), xsl(tau, 1, W1), vcol(VC_SUG + tau)),
                waits=[ev_dbg_xn] if (DBG and tau == 0) else [], inc=s_vp))

        ev_f1 = []
        for tau in range(DT):
            for th in range(TH):
                j = tau * TH + th
                pf = psF[j % 2]
                waits = [ev_xt[tau]]
                if j >= 2:
                    waits.append((s_ev, cnt.get(s_ev, 0) - 1))
                evm = op("tensor", lambda t, tau=tau, th=th, pf=pf: t.matmul(
                    pf[:, 0:512], B_hm[:, HM_H128:HM_H128 + 128],
                    xsl(tau, 1 + th * 512, 1 + th * 512 + 512),
                    start=True, stop=True), waits=waits, inc=s_pe)
                ev_f1.append(op("scalar", lambda s, tau=tau, th=th, pf=pf: s.activation(
                    xsl(tau, 1 + th * 512, 1 + th * 512 + 512), pf[:, 0:512],
                    AF.Identity), waits=[evm], inc=s_ev))

        ev_sho = []
        for i in range(4):
            ev_sho.append(op("sync", lambda s, i=i: s.dma_start(
                out=yb2k[:, i * 4:(i + 1) * 4, :],
                in_=AR1[:].rearrange("p (tau t) -> p tau t", t=W1)[:, i * 4:(i + 1) * 4, 1:W1]),
                waits=[ev_f1[-1]] if i == 0 else [], inc=s_sh, amt=16))
        yb2k_sh = yb2k.rearrange("(h l) tau t -> h tau l t", h=8, l=16)
        ev_shi = []
        for h in range(8):
            ev_shi.append(op("sync", lambda s, h=h: s.dma_start(
                out=AR1[h * 16:(h + 1) * 16, 0:16 * Tc].rearrange("p (l t) -> p l t", t=Tc),
                in_=yb2k_sh[h]),
                waits=[(s_sh, 64)] if h == 0 else [], inc=s_sh, amt=16))

        ev_x1 = []
        for j in range(DT * Tc // 512):
            pf = psF[j % 2]
            waits = [(s_sh, 192)] if j == 0 else []
            if j >= 2:
                waits.append((s_ev, cnt.get(s_ev, 0) - 1))
            evm = op("tensor", lambda t, j=j, pf=pf: t.matmul(
                pf[:, 0:512], B_hm[:, HM_I8H16:HM_I8H16 + 128],
                x1sl(j * 512, (j + 1) * 512), start=True, stop=True),
                waits=waits, inc=s_pe)
            ev_x1.append(op("scalar", lambda s, j=j, pf=pf: s.activation(
                x1sl(j * 512, (j + 1) * 512), pf[:, 0:512], AF.Identity,
                scale=S2048), waits=[evm], inc=s_ev))
        ev_x1_done = ev_x1[-1]
        if DBG:
            ev_x1_done = op("sync", lambda s: s.dma_start(out=dbg_x1, in_=AR1[:, 0:DT * Tc]),
                            waits=[ev_x1[-1]], inc=s_dbg, amt=16)

        P2_TB, P2_L = 0, Tc
        P2_E1 = [2 * Tc, 3 * Tc]
        P2_AO = [4 * Tc, 5 * Tc]
        P2_BO = [6 * Tc, 7 * Tc]
        Q_E2 = [0, Tc]
        Q_A = [2 * Tc, 3 * Tc]
        ev_spe1, ev_spa, ev_spe2 = {}, {}, {}
        ev_scan = {}
        for tau in range(DT):
            par = tau % 2
            waits = [ev_kk[(tau + 1) * TH - 1], ev_vv[(tau + 1) * TH - 1]]
            ev_tb = op("vector", lambda v, tau=tau: v.tensor_scalar(
                scr(P2_TB, Tc), kksl(tau, 0, Tc), vcol(VC_W + tau), None, ALU.max),
                waits=waits)
            ev_l = op("vector", lambda v, tau=tau: v.tensor_scalar(
                scr(P2_L, Tc), scr(P2_TB, Tc), -1.0, vcol(VC_W + tau),
                ALU.mult, ALU.subtract), inc=s_vp)
            op("vector", lambda v, tau=tau: v.reduce_sum(
                acol(AX_RTMP + tau), scr(P2_L, Tc), mybir.AxisListType.X))
            waits = [ev_l]
            if tau >= 2:
                waits.append(ev_spe1[tau - 2])
            ev_e1 = op("scalar", lambda s, par=par: s.activation(
                scr(P2_E1[par], Tc), scr(P2_L, Tc), AF.Exp), waits=waits, inc=s_sc)
            ev_sub = op("vector", lambda v, tau=tau: v.tensor_sub(
                scr(P2_TB, Tc), kksl(tau, 0, Tc), scr(P2_TB, Tc)), inc=s_vp)
            waits = [ev_sub]
            if tau >= 2:
                waits.append(ev_spe2[tau - 2])
            ev_e2 = op("scalar", lambda s, par=par: s.activation(
                VQ[:, Q_E2[par]:Q_E2[par] + Tc], scr(P2_TB, Tc), AF.Exp),
                waits=waits, inc=s_sc)
            waits = [ev_e2]
            if tau >= 2:
                waits.append(ev_spa[tau - 2])
            ev_A = op("vector", lambda v, tau=tau, par=par: v.tensor_mul(
                VQ[:, Q_A[par]:Q_A[par] + Tc], VQ[:, Q_E2[par]:Q_E2[par] + Tc],
                vsl(tau, 0, Tc)), waits=waits, inc=s_vp)
            op("vector", lambda v, par=par: v.tensor_tensor_scan(
                scr(P2_AO[par], Tc), scr(P2_E1[par], Tc),
                VQ[:, Q_A[par]:Q_A[par] + Tc], 0.0, ALU.mult, ALU.add))
            ev_sb = op("vector", lambda v, par=par: v.tensor_tensor_scan(
                scr(P2_BO[par], Tc), scr(P2_E1[par], Tc),
                VQ[:, Q_E2[par]:Q_E2[par] + Tc], 0.0, ALU.mult, ALU.add),
                inc=s_vp)
            ev_scan[tau] = ev_sb
            if tau >= 1:
                op("vector", lambda v, tau=tau, par=par: v.tensor_copy(
                    acol(AX_ACAR + tau - 1),
                    scr(P2_AO[1 - par], Tc)[:, Tc - 1:Tc]))
                op("vector", lambda v, tau=tau, par=par: v.tensor_copy(
                    acol(AX_BCAR + tau - 1),
                    scr(P2_BO[1 - par], Tc)[:, Tc - 1:Tc]))
            ev_spe1[tau] = op("sync", lambda s, tau=tau, par=par: s.dma_start(
                out=e1sp[tau], in_=scr(P2_E1[par], Tc)), waits=[ev_e1],
                inc=s_spe1, amt=16)
            ev_spe2[tau] = op("sync", lambda s, tau=tau, par=par: s.dma_start(
                out=e2sp[tau], in_=VQ[:, Q_E2[par]:Q_E2[par] + Tc]),
                waits=[ev_e2], inc=s_spe2, amt=16)
            ev_spa[tau] = op("sync", lambda s, tau=tau, par=par: s.dma_start(
                out=asp[tau], in_=VQ[:, Q_A[par]:Q_A[par] + Tc]),
                waits=[ev_scan[tau]], inc=s_spa, amt=16)

        ev_psum = op("vector", lambda v: v.tensor_add(
            B_aux[:, AX_PDEC:AX_PDEC + 16], B_aux[:, AX_PDEC:AX_PDEC + 16],
            B_aux[:, AX_RTMP:AX_RTMP + 16]), inc=s_vp)
        ev_pexp = op("scalar", lambda s: s.activation(
            B_aux[:, AX_PDEC:AX_PDEC + 16], B_aux[:, AX_PDEC:AX_PDEC + 16],
            AF.Exp), waits=[ev_psum], inc=s_sc)
        op("vector", lambda v: v.tensor_copy(
            acol(AX_ACAR + DT - 1), scr(P2_AO[(DT - 1) % 2], Tc)[:, Tc - 1:Tc]),
            waits=[ev_pexp])
        ev_lc15 = op("vector", lambda v: v.tensor_copy(
            acol(AX_BCAR + DT - 1), scr(P2_BO[(DT - 1) % 2], Tc)[:, Tc - 1:Tc]),
            inc=s_vp)

        ev_lc15s = vslack([ev_lc15])
        ev_put = op("sync", lambda s: s.dma_start(
            out=cc_in, in_=B_aux[:, AX_PDEC:AX_PDEC + 48]),
            waits=[ev_lc15s, ev_pexp], inc=s_put, amt=16)
        ev_gth = op("sync", lambda s: s.dma_start(
            out=B_gath[:].rearrange("p (j f) -> p j f", f=48),
            in_=cc_out.rearrange("(j p) f -> p j f", p=128)),
            waits=[(s_cc, 1)], inc=s_gt, amt=16)

        prev_cb = [ev_gth]
        def cop(action):
            ev = op("vector", action, waits=list(prev_cb), inc=s_vp)
            prev_cb[:] = [ev]
        for j in range(NCORES - 1):
            Ft = B_aux[:, AX_FT:AX_FT + 16]
            Tt = B_aux[:, AX_TT:AX_TT + 16]
            Tb = B_aux[:, AX_TT + 16:AX_TT + 32]
            cop(lambda v, j=j, Ft=Ft: v.tensor_scalar(
                Ft, B_gath[:, j * 48:j * 48 + 16], vcol(VC_PM + j), vcol(VC_PMI + j),
                ALU.mult, ALU.add))
            cop(lambda v, j=j, Tt=Tt: v.tensor_scalar(
                Tt, B_gath[:, j * 48 + 16:j * 48 + 32], vcol(VC_PM + j), None, ALU.mult))
            cop(lambda v, j=j, Tb=Tb: v.tensor_scalar(
                Tb, B_gath[:, j * 48 + 32:j * 48 + 48], vcol(VC_PM + j), None, ALU.mult))
            cop(lambda v, Ft=Ft: v.tensor_mul(
                B_aux[:, AX_A0:AX_A0 + 16], B_aux[:, AX_A0:AX_A0 + 16], Ft))
            cop(lambda v, Tt=Tt: v.tensor_add(
                B_aux[:, AX_A0:AX_A0 + 16], B_aux[:, AX_A0:AX_A0 + 16], Tt))
            cop(lambda v, Ft=Ft: v.tensor_mul(
                B_aux[:, AX_B0:AX_B0 + 16], B_aux[:, AX_B0:AX_B0 + 16], Ft))
            cop(lambda v, Tb=Tb: v.tensor_add(
                B_aux[:, AX_B0:AX_B0 + 16], B_aux[:, AX_B0:AX_B0 + 16], Tb))
        ev_comb_v = vslack([last(s_vp)])
        ev_comb = op("sync", lambda s: s.dma_start(
            out=B_aux[:, AX_A02:AX_A02 + 32], in_=B_aux[:, AX_A0:AX_A0 + 32]),
            waits=[ev_comb_v], inc=s_cbc, amt=16)

        if DBG:
            ev_dbg_cb = op("sync", lambda s: s.dma_start(out=dbg_aux, in_=B_aux[:]),
                           waits=[ev_comb], inc=s_dbg, amt=16)
            op("sync", lambda s: s.dma_start(out=dbg_gath, in_=B_gath[:]),
               inc=s_dbg, amt=16)

        ev_used = {}
        for tau in range(DT):
            par = tau % 2
            waits = []
            if tau < 2:
                waits += [ev_lc15, (s_spe1, 16 * DT), (s_spa, 16 * DT),
                          (s_spe2, 16 * DT)]
            else:
                waits.append(ev_used[tau - 2])
            srl = s_rl if par == 0 else s_rlb
            ev_r1 = op("sync", lambda s, tau=tau, par=par: s.dma_start(
                out=scr(P2_E1[par], Tc), in_=e1sp[tau]), waits=waits,
                inc=srl, amt=16)
            ev_r2 = op("sync", lambda s, tau=tau, par=par: s.dma_start(
                out=VQ[:, Q_A[par]:Q_A[par] + Tc], in_=asp[tau]), inc=srl, amt=16)
            ev_r3 = op("sync", lambda s, tau=tau, par=par: s.dma_start(
                out=VQ[:, Q_E2[par]:Q_E2[par] + Tc], in_=e2sp[tau]), inc=srl, amt=16)
            waits = [ev_r3]
            if tau == 0:
                waits.append(ev_comb)
            op("vector", lambda v, tau=tau, par=par: v.tensor_tensor_scan(
                scr(P2_AO[par], Tc), scr(P2_E1[par], Tc),
                VQ[:, Q_A[par]:Q_A[par] + Tc], acol(AX_A02 + tau),
                ALU.mult, ALU.add), waits=waits)
            op("vector", lambda v, tau=tau, par=par: v.tensor_tensor_scan(
                scr(P2_BO[par], Tc), scr(P2_E1[par], Tc),
                VQ[:, Q_E2[par]:Q_E2[par] + Tc], acol(AX_A02 + 16 + tau),
                ALU.mult, ALU.add), waits=[ev_r3])
            op("vector", lambda v, par=par: v.reciprocal(
                scr(P2_BO[par], Tc), scr(P2_BO[par], Tc)))
            ev_used[tau] = op("vector", lambda v, tau=tau, par=par: v.tensor_mul(
                wkvsl(tau, 0, Tc), scr(P2_AO[par], Tc), scr(P2_BO[par], Tc)),
                inc=s_vp)
        ev_wkv_done = ev_used[DT - 1]
        if DBG:
            ev_wkv_done = op("sync", lambda s: s.dma_start(out=dbg_wkv, in_=ARZ[:, 0:DT * Tc]),
                             waits=[ev_used[DT - 1]], inc=s_dbg, amt=16)

        UPLOOK = 3
        ev_upld = {}
        for tf in range(min(UPLOOK, FT)):
            ev_upld[(tf, 0)] = wload(wup[tf], 1)
            ev_upld[(tf, 1)] = wload(wgt[tf], 1)
        op("gpsimd", lambda g: g.collective_compute(
            "AllGather", ALU.bypass, replica_groups=[list(range(NCORES))],
            ins=[cc_in.opt()], outs=[cc_out.opt()]),
            waits=[ev_put], inc=s_cc, amt=0)

        ev_zsp = {}
        ev_zm = {}
        ev_st1 = {}
        ev_ug_evac = {}

        def zview(tfx):
            par = tfx % 2
            return B_ug[:, (par * 3 + 2) * Tc:(par * 3 + 2) * Tc + Tc]

        for tf in range(FT + 2):
            tfp = tf - 1
            if 0 <= tfp < FT:
                tz = zview(tfp)
                for th in range(TH):
                    pf = psF[th % 2]
                    waits = [ev_zm[(tfp, th)], (s_ev, cnt.get(s_ev, 0) - 1)]
                    evm = op("tensor", lambda t, th=th, pf=pf, tz=tz: t.matmul(
                        pf[:, 0:512], B_hm[:, HM_H128:HM_H128 + 128],
                        tz[:, th * 512:th * 512 + 512], start=True, stop=True),
                        waits=waits, inc=s_pe)
                    ev_st1[(tfp, th)] = op("scalar", lambda s, th=th, pf=pf, tz=tz:
                        s.activation(tz[:, th * 512:th * 512 + 512], pf[:, 0:512],
                                     AF.Identity), waits=[evm], inc=s_ev)
            tfs = tf - 2
            if 0 <= tfs < FT:
                tz = zview(tfs)
                ev_zsp[tfs] = op("scalar", lambda s, tfs=tfs, tz=tz: s.dma_start(
                    out=yb8k[:, tfs, :], in_=tz),
                    waits=[ev_st1[(tfs, TH - 1)]], inc=s_za[tfs % 2], amt=16)
            if tf >= FT:
                continue
            par = tf % 2
            tmpU = B_ug[:, (par * 3 + 0) * Tc:(par * 3 + 0) * Tc + Tc]
            tmpG = B_ug[:, (par * 3 + 1) * Tc:(par * 3 + 1) * Tc + Tc]
            tmpZ = zview(tf)
            for wi, (tmp, sc_c, func) in enumerate(
                    ((tmpU, AX_UPS, AF.Identity), (tmpG, AX_GSC64, AF.Silu))):
                wview, wevt, nsl = wgrab()
                for th in range(TH):
                    gidx = (tf * 2 + wi) * TH + th
                    pbank = ps[gidx % 4]
                    for l in range(16):
                        waits = []
                        if l == 0:
                            waits.append(wevt)
                            waits.append((s_ev, cnt.get(s_ev, 0) - 3))
                            if tf == 0 and wi == 0 and th == 0:
                                waits.append(ev_x1_done)
                        op("tensor", lambda t, l=l, th=th, pbank=pbank, wview=wview: t.matmul(
                            pbank[:, 0:512], wview[:, l * 128:(l + 1) * 128],
                            x1sl(l * Tc + th * 512, l * Tc + th * 512 + 512),
                            start=(l == 0), stop=(l == 15)),
                           waits=waits, inc=(s_pe if l == 15 else None))
                    ev_grp = last(s_pe)
                    waits = [ev_grp]
                    if tf >= 2 and th == 0 and wi == 0:
                        waits.append(ev_zm[(tf - 2, TH - 1)])
                    ev_ug_evac[(tf, wi, th)] = op("scalar",
                        lambda s, th=th, tmp=tmp, pbank=pbank, sc_c=sc_c, func=func, tf=tf:
                        s.activation(tmp[:, th * 512:th * 512 + 512], pbank[:, 0:512],
                                     func, scale=acol(sc_c + tf)),
                        waits=waits, inc=s_ev)
                wrelease(nsl)
            for th in range(TH):
                waits = [ev_ug_evac[(tf, 0, th)], ev_ug_evac[(tf, 1, th)]]
                if tf >= 2 and th == 0:
                    waits.append(ev_zsp.get(tf - 2))
                ev_zm[(tf, th)] = op("vector", lambda v, th=th, tmpU=tmpU, tmpG=tmpG,
                                     tmpZ=tmpZ: v.tensor_mul(
                    tmpZ[:, th * 512:th * 512 + 512], tmpG[:, th * 512:th * 512 + 512],
                    tmpU[:, th * 512:th * 512 + 512]), waits=waits, inc=s_vp)
            if tf + UPLOOK < FT:
                ev_upld[(tf + UPLOOK, 0)] = wload(wup[tf + UPLOOK], 1)
                ev_upld[(tf + UPLOOK, 1)] = wload(wgt[tf + UPLOOK], 1)

        for t_ in range(DT):
            wload(wr[t_], 1)
        ev_rld = []
        for i in range(4):
            waits = []
            if i == 0:
                waits = [ev_wkv_done, last(s_rsp[0]), last(s_rsp[1]),
                         last(s_rl), last(s_rlb)]
            ev_rld.append(op("sync", lambda s, i=i: s.dma_start(
                out=VRIN[:, i * 4 * Tc:(i + 1) * 4 * Tc],
                in_=rinsp[:, i * 4 * Tc:(i + 1) * 4 * Tc]),
                waits=waits, inc=s_ri, amt=16))

        ev_wkvr = []
        for e in range(DT):
            wview, wevt, nsl = wgrab()
            for th in range(TH):
                gidx = e * TH + th
                pbank = ps[gidx % 4]
                for dtl in range(DT):
                    waits = []
                    if dtl == 0:
                        waits.append(wevt)
                        waits.append((s_ev, cnt.get(s_ev, 0) - 3))
                        if gidx == 0:
                            waits.append((s_ri, 64))
                    op("tensor", lambda t, th=th, dtl=dtl, pbank=pbank, wview=wview: t.matmul(
                        pbank[:, 0:512], wview[:, dtl * 128:(dtl + 1) * 128],
                        VRIN[:, dtl * Tc + th * 512: dtl * Tc + th * 512 + 512],
                        start=(dtl == 0), stop=(dtl == DT - 1)),
                       waits=waits, inc=(s_pe if dtl == DT - 1 else None))
                ev_grp = last(s_pe)
                sl = VEV[:, (gidx % 2) * 512:(gidx % 2) * 512 + 512]
                waits = [ev_grp]
                if gidx == 0:
                    waits.append(last(s_za[0]))
                    waits.append(last(s_za[1]))
                    waits.append(ev_zm[(FT - 1, TH - 1)])
                    waits.append(ev_zm[(FT - 2, TH - 1)])
                if gidx >= 2:
                    waits.append(ev_wkvr[gidx - 2])
                eve = op("scalar", lambda s, sl=sl, pbank=pbank, e=e: s.activation(
                    sl, pbank[:, 0:512], AF.Identity, bias=vcol(VC_BR + e)),
                    waits=waits, inc=s_ev)
                waits = [eve]
                if gidx == 0:
                    waits.append(ev_wkv_done)
                ev_wkvr.append(op("vector", lambda v, e=e, th=th, sl=sl: v.tensor_mul(
                    wkvr[:, e * Tc + th * 512: e * Tc + th * 512 + 512], sl,
                    wkvsl(e, th * 512, th * 512 + 512)), waits=waits, inc=s_vp))
            wrelease(nsl)

        if DBG:
            op("sync", lambda s: s.dma_start(out=dbg_wkvr, in_=AR2[:]),
               waits=[ev_wkvr[-1]], inc=s_dbg, amt=16)
        for rep in range(max(2, TH)):
            for to in range(DT):
                wload(wdn[to * 2 + 0], 2)
                wload(wdn[to * 2 + 1], 2)
        for t_ in range(DT):
            wload(wo[t_], 1)

        yb8k_sh = yb8k.rearrange("(h l) tau t -> h tau l t", h=2, l=64)
        NW = max(2, TH)
        TW = Tc // NW
        ev_g = []
        for th2 in range(NW):
            ev_w = []
            for h in range(2):
                waits = []
                if h == 0:
                    waits.append(last(s_za[0]))
                    waits.append(last(s_za[1]))
                    if th2 == 0:
                        waits.append(ev_wkvr[-1])
                    else:
                        waits.append(last(s_pe))
                ev_w.append(op("sync", lambda s, h=h, th2=th2: s.dma_start(
                    out=ARZ[h * 64:(h + 1) * 64, 0:64 * TW]
                        .rearrange("p (l t) -> p l t", t=TW),
                    in_=yb8k_sh[h, :, :, th2 * TW:th2 * TW + TW]),
                    waits=waits, inc=s_zw, amt=16))
            ev_s2 = []
            for j in range(64 * TW // 512):
                pf = psF[j % 2]
                waits = [(s_zw, 32 * (th2 + 1))] if j == 0 else []
                if j >= 2:
                    waits.append((s_ev, cnt.get(s_ev, 0) - 1))
                evm = op("tensor", lambda t, j=j, pf=pf: t.matmul(
                    pf[:, 0:512], B_hm[:, HM_I2H64:HM_I2H64 + 128],
                    z3sl(j * 512, (j + 1) * 512), start=True, stop=True),
                    waits=waits, inc=s_pe)
                ev_s2.append(op("scalar", lambda s, j=j, pf=pf: s.activation(
                    z3sl(j * 512, (j + 1) * 512), pf[:, 0:512], AF.Identity,
                    scale=S8192), waits=[evm], inc=s_ev))
            for to in range(DT):
                wv0, wevt0, n0 = wgrab()
                wv1, wevt1, n1 = wgrab()
                pbank = ps[to % 4]
                for l in range(64):
                    wview = wv0 if l < 32 else wv1
                    loc = l if l < 32 else l - 32
                    waits = []
                    if l == 0:
                        waits += [wevt0, wevt1, ev_s2[-1],
                                  (s_ev, cnt.get(s_ev, 0) - 3)]
                    op("tensor", lambda t, l=l, loc=loc, pbank=pbank, wview=wview: t.matmul(
                        pbank[:, 0:TW], wview[:, loc * 128:(loc + 1) * 128],
                        z3sl(l * TW, (l + 1) * TW),
                        start=(l == 0), stop=(l == 63)),
                       waits=waits, inc=(s_pe if l == 63 else None))
                ev_grp = last(s_pe)
                ev_g.append(op("scalar", lambda s, to=to, th2=th2, pbank=pbank: s.activation(
                    gsl(to, th2 * TW, th2 * TW + TW), pbank[:, 0:TW], AF.Silu,
                    scale=acol(AX_WDP64 + to)), waits=[ev_grp], inc=s_ev))
                wrelease(n0 + n1)
        ev_g_done = ev_g[-1]

        if DBG:
            ev_g_done = op("sync", lambda s: s.dma_start(
                out=dbg_g, in_=AR1[:, 0:DT * Tc]), waits=[ev_g[-1]],
                inc=s_dbg, amt=16)
        ev_wkvr_s = vslack([ev_wkvr[-1]])
        xf_ev = {}
        mul_ev = {}
        comb_ev = {}
        st_ev = {}
        for e in range(DT):
            wview, wevt, nsl = wgrab()
            for th in range(TH):
                idx = e * TH + th
                xf_slot = scr(SC_XF, 2048)[:, (idx % 2) * 512:(idx % 2) * 512 + 512]
                waits = []
                if idx == 0:
                    waits.append(ev_wkv_done)
                    waits.append((s_ri, 64))
                    waits.append(ev_wkvr[-1])
                if idx >= 2:
                    waits.append(comb_ev[idx - 2])
                xf_ev[idx] = op("sync", lambda s, e=e, th=th, xf_slot=xf_slot: s.dma_start(
                    out=xf_slot, in_=xin_r[:, e, 1 + th * 512:1 + th * 512 + 512]),
                    waits=waits, inc=s_xf[idx % 2], amt=16)
                pbank = ps[idx % 4]
                for dtl in range(DT):
                    mwaits = []
                    if dtl == 0:
                        mwaits.append(wevt)
                        mwaits.append((s_ev, cnt.get(s_ev, 0) - 3))
                        if idx == 0:
                            mwaits.append(ev_wkvr_s)
                    op("tensor", lambda t, th=th, dtl=dtl, pbank=pbank, wview=wview: t.matmul(
                        pbank[:, 0:512], wview[:, dtl * 128:(dtl + 1) * 128],
                        wkvr[:, dtl * Tc + th * 512: dtl * Tc + th * 512 + 512],
                        start=(dtl == 0), stop=(dtl == DT - 1)),
                       waits=mwaits, inc=(s_pe if dtl == DT - 1 else None))
                ev_grp = last(s_pe)
                sl = VEV[:, (idx % 2) * 512:(idx % 2) * 512 + 512]
                waits = [ev_grp]
                if idx >= 2:
                    waits.append(mul_ev[idx - 2])
                eve = op("scalar", lambda s, sl=sl, pbank=pbank, e=e: s.activation(
                    sl, pbank[:, 0:512], AF.Identity, bias=vcol(VC_BO + e)),
                    waits=waits, inc=s_ev)
                ot = scr(SC_OT, 2048)[:, (idx % 2) * 512:(idx % 2) * 512 + 512]
                waits = [eve, xf_ev[idx]]
                if idx == 0:
                    waits.append(ev_g_done)
                if idx >= 3:
                    waits.append(st_ev[idx - 2])
                elif idx == 2:
                    waits.append(st_ev[0])
                mul_ev[idx] = op("vector", lambda v, sl=sl, ot=ot, e=e, th=th: v.tensor_mul(
                    ot, sl, gsl(e, th * 512, th * 512 + 512)), waits=waits, inc=s_vp)
                comb_ev[idx] = op("vector", lambda v, ot=ot, xf_slot=xf_slot: v.tensor_add(
                    ot, ot, xf_slot), inc=s_vp)
                if idx >= 1:
                    pidx = idx - 1
                    pot = scr(SC_OT, 2048)[:, (pidx % 2) * 512:(pidx % 2) * 512 + 512]
                    pe_, pth = pidx // TH, pidx % TH
                    st_ev[pidx] = op("sync", lambda s, pe_=pe_, pth=pth, pot=pot: s.dma_start(
                        out=out[:, pe_ * Tc + pth * 512: pe_ * Tc + pth * 512 + 512],
                        in_=pot), waits=[comb_ev[pidx]], inc=s_st[pidx % 2], amt=16)
            wrelease(nsl)

        lidx = DT * TH - 1
        ev_lsl = vslack([comb_ev[lidx]])
        st_ev[lidx] = op("sync", lambda s: s.dma_start(
            out=out[:, (DT - 1) * Tc + (TH - 1) * 512: DT * Tc], 
            in_=scr(SC_OT, 2048)[:, (lidx % 2) * 512:(lidx % 2) * 512 + 512]),
            waits=[ev_lsl], inc=s_st[lidx % 2], amt=16)
        op("sync", lambda s: s.nop(), waits=[last(s_st[0]), last(s_st[1])])
        op("gpsimd", lambda g: g.nop(), waits=[last(s_st[0]), last(s_st[1])])

        assert wstate["grab"] == len(wblocks), (wstate, len(wblocks))

        block = E(nc.Block())

        def resolve_defer(marker):
            need = wdefer[marker]
            for cum, ev in wreleases:
                if cum >= need:
                    return ev
            raise AssertionError("unresolvable ring release")

        def make_runner(items):
            def runner(eng):
                for waits, action, inc_sem, amt in items:
                    for wt in waits:
                        if isinstance(wt[0], str) and wt[0] == "DEFER_REL":
                            wt = resolve_defer(wt)
                        (sem, val) = wt
                        if val > 0:
                            eng.wait_ge(sem, val)
                    ins = action(eng)
                    if inc_sem is not None:
                        if amt:
                            ins.then_inc(inc_sem, amt)
                        else:
                            ins.then_inc(inc_sem)
            return runner

        block.sync(make_runner(plan["sync"]))
        block.gpsimd(make_runner(plan["gpsimd"]))
        block.vector(make_runner(plan["vector"]))
        block.scalar(make_runner(plan["scalar"]))
        block.tensor(make_runner(plan["tensor"]))

    return nc



def _H(n):
    h = np.array([[1.0]], dtype=np.float32)
    while h.shape[0] < n:
        h = np.block([[h, h], [h, -h]])
    return h


def _pack_attn(W):
    WT = np.ascontiguousarray(np.asarray(W, np.float32).T)
    return np.ascontiguousarray(
        WT.reshape(16, 128, 16, 128).transpose(2, 1, 0, 3).reshape(16, 128, 16 * 128))


def _pack_upgate(W):
    W = np.asarray(W, np.float32)
    P2 = np.arange(128)
    l = np.arange(16)
    dmap = (P2[:, None] % 16) * 128 + (P2[:, None] // 16) * 16 + l[None, :]
    A = W.T[dmap, :]
    return np.ascontiguousarray(
        A.reshape(128, 16, 64, 128).transpose(2, 0, 1, 3).reshape(64, 128, 16 * 128))


def _pack_down(W):
    W = np.asarray(W, np.float32)
    P3 = np.arange(128)
    l = np.arange(64)
    fmap = (P3[:, None] % 64) * 128 + (P3[:, None] // 64) * 64 + l[None, :]
    B = W.T[fmap, :]
    Bb = B.reshape(128, 2, 32, 16, 128).transpose(3, 1, 0, 2, 4)
    return np.ascontiguousarray(Bb.reshape(16 * 2, 128, 32 * 128))


def _v2(vec):
    return np.ascontiguousarray(np.asarray(vec, np.float32).reshape(-1, 128).T)


def kernel(L0_attn_x, L0_attn_Wk, L0_attn_bk, L0_attn_Wv, L0_attn_bv,
           L0_attn_Wr, L0_attn_br, L0_attn_Wo, L0_attn_bo, L0_attn_tmk,
           L0_attn_tmv, L0_attn_tmr, L0_attn_u, L0_attn_w, L0_ffn_SU_ug,
           L0_ffn_SU_dp, L0_ffn_Wscale_ug, L0_ffn_Wscale_dp, L0_ffn_W_up,
           L0_ffn_W_gate, L0_ffn_W_down):
    T = L0_attn_x.shape[1]
    Tc = T // NCORES
    C = 128
    x = np.asarray(L0_attn_x, np.float32)[0]
    xT = np.ascontiguousarray(x.T)

    wk = _pack_attn(L0_attn_Wk)
    wvv = _pack_attn(L0_attn_Wv)
    wrr = _pack_attn(L0_attn_Wr)
    woo = _pack_attn(L0_attn_Wo)
    wupp = _pack_upgate(L0_ffn_W_up)
    wgtt = _pack_upgate(L0_ffn_W_gate)
    wdnn = _pack_down(L0_ffn_W_down)

    hm = np.zeros((128, 385), np.float32)
    hm[:, 0:128] = _H(128)
    hm[:, 128:256] = np.kron(np.eye(8, dtype=np.float32), _H(16))
    hm[:, 256:384] = np.kron(np.eye(2, dtype=np.float32), _H(64))
    hm[:, 384] = 1.0
    hm = hm.astype(ml_dtypes.bfloat16)

    in_maps = []
    for c in range(NCORES):
        t0 = c * Tc
        prev = np.zeros((D, 1), np.float32) if c == 0 else xT[:, t0 - 1:t0]
        xin = np.ascontiguousarray(np.concatenate([prev, xT[:, t0:t0 + Tc]], axis=1))
        vv = np.zeros((128, 512), np.float32)
        vv[:, VC_TMK:VC_TMK + 16] = _v2(L0_attn_tmk)
        vv[:, VC_TMV:VC_TMV + 16] = _v2(L0_attn_tmv)
        vv[:, VC_TMR:VC_TMR + 16] = _v2(L0_attn_tmr)
        vv[:, VC_W:VC_W + 16] = _v2(L0_attn_w)
        vv[:, VC_U:VC_U + 16] = _v2(L0_attn_u)
        vv[:, VC_BK:VC_BK + 16] = _v2(L0_attn_bk)
        vv[:, VC_BV:VC_BV + 16] = _v2(L0_attn_bv)
        vv[:, VC_BR:VC_BR + 16] = _v2(L0_attn_br)
        vv[:, VC_BO:VC_BO + 16] = _v2(L0_attn_bo)
        vv[:, VC_SUG:VC_SUG + 16] = _v2(L0_ffn_SU_ug)
        vv[:, VC_WDP:VC_WDP + 16] = _v2(L0_ffn_Wscale_dp)
        vv[:, VC_WSU:VC_WSU + 64] = _v2(np.asarray(L0_ffn_Wscale_ug)[:FF])
        vv[:, VC_SUDP:VC_SUDP + 64] = _v2(L0_ffn_SU_dp)
        vv[:, VC_GSC:VC_GSC + 64] = _v2(np.asarray(L0_ffn_Wscale_ug)[FF:])
        vv[:, VC_PM:VC_PM + 8] = (np.arange(8)[None, :] < c).astype(np.float32)
        vv[:, VC_PMI:VC_PMI + 8] = 1.0 - vv[:, VC_PM:VC_PM + 8]
        vv[:, VC_ONES:VC_ONES + 128] = 1.0
        in_maps.append({
            "xin": xin, "wk": wk, "wv": wvv, "wr": wrr, "wo": woo,
            "wup": wupp, "wgt": wgtt, "wdn": wdnn, "vecs": vv, "hmat": hm,
        })

    nc = build_program(Tc, C)
    res = run_bass_kernel_spmd(nc, in_maps, core_ids=list(range(NCORES)))

    outs = []
    for c in range(NCORES):
        oc = np.asarray(res.results[c]["out"])
        outs.append(oc.reshape(128, DT, Tc).transpose(1, 0, 2).reshape(D, Tc).T)
    return np.ascontiguousarray(np.concatenate(outs, axis=0)[None]).astype(np.float32)


# version 11
# speedup vs baseline: 1.0826x; 1.0826x over previous
import contextlib
import numpy as np
import ml_dtypes
import concourse.bass as bass
import concourse.mybir as mybir
from concourse.bass_utils import run_bass_kernel_spmd

F32 = mybir.dt.float32
BF16 = mybir.dt.bfloat16
AF = mybir.ActivationFunctionType
ALU = mybir.AluOpType

DBG = False
NCORES = 8
D = 2048
DT = 16
FF = 8192
FT = 64
S2048 = float(1.0 / (np.sqrt(2048.0) * 64.0))
S8192 = float(1.0 / (np.sqrt(8192.0) * 64.0))

VC_TMK, VC_TMV, VC_TMR = 0, 16, 32
VC_W, VC_U, VC_BK, VC_BV, VC_BR, VC_BO = 48, 64, 80, 96, 112, 128
VC_SUG, VC_WDP = 144, 160
VC_WSU, VC_SUDP, VC_GSC = 176, 240, 304
VC_PM, VC_PMI = 368, 376
VC_ONES = 384
AX_BKU, AX_WDP64, AX_PDEC, AX_ACAR, AX_BCAR = 0, 16, 32, 48, 64
AX_A0, AX_B0, AX_UPS, AX_GSC64, AX_RTMP, AX_FT, AX_TT = 80, 96, 112, 176, 240, 256, 272
AX_ACAR2, AX_BCAR2, AX_A02 = 288, 304, 320
HM_H128, HM_I8H16, HM_I2H64, HM_ONE = 0, 128, 256, 384


def build_program(Tc, C):
    NCH = Tc // C
    TH = Tc // 512
    W1 = Tc + 1
    assert Tc % 512 == 0 and C * NCH == Tc and NCH >= 2

    nc = bass.Bass("TRN2", target_bir_lowering=False, detect_race_conditions=False)

    xin = nc.dram_tensor("xin", [D, W1], F32, kind="ExternalInput").ap()
    wk = nc.dram_tensor("wk", [DT, 128, DT * 128], F32, kind="ExternalInput").ap()
    wv = nc.dram_tensor("wv", [DT, 128, DT * 128], F32, kind="ExternalInput").ap()
    wr = nc.dram_tensor("wr", [DT, 128, DT * 128], F32, kind="ExternalInput").ap()
    wo = nc.dram_tensor("wo", [DT, 128, DT * 128], F32, kind="ExternalInput").ap()
    wup = nc.dram_tensor("wup", [FT, 128, 16 * 128], F32, kind="ExternalInput").ap()
    wgt = nc.dram_tensor("wgt", [FT, 128, 16 * 128], F32, kind="ExternalInput").ap()
    wdn = nc.dram_tensor("wdn", [DT * 2, 128, 32 * 128], F32, kind="ExternalInput").ap()
    vecs = nc.dram_tensor("vecs", [128, 512], F32, kind="ExternalInput").ap()
    hmat = nc.dram_tensor("hmat", [128, 385], BF16, kind="ExternalInput").ap()
    out = nc.dram_tensor("out", [128, DT * Tc], F32, kind="ExternalOutput").ap()

    e1sp = nc.dram_tensor("e1sp", [DT, 128, Tc], F32, kind="Internal").ap()
    asp = nc.dram_tensor("asp", [DT, 128, Tc], BF16, kind="Internal").ap()
    e2sp = nc.dram_tensor("e2sp", [DT, 128, Tc], BF16, kind="Internal").ap()
    rinsp = nc.dram_tensor("rinsp", [128, DT * Tc], BF16, kind="Internal").ap()
    yb2k = nc.dram_tensor("yb2k", [128, DT, Tc], BF16, kind="Internal").ap()
    yb8k = nc.dram_tensor("yb8k", [128, FT, Tc], BF16,
                          kind=("ExternalOutput" if DBG else "Internal")).ap()
    if DBG:
        dbg_xn = nc.dram_tensor("dbg_xn", [128, DT * W1], BF16, kind="ExternalOutput").ap()
        dbg_kk = nc.dram_tensor("dbg_kk", [128, DT * Tc], BF16, kind="ExternalOutput").ap()
        dbg_v = nc.dram_tensor("dbg_v", [128, DT * Tc], BF16, kind="ExternalOutput").ap()
        dbg_kin = nc.dram_tensor("dbg_kin", [128, DT * Tc], BF16, kind="ExternalOutput").ap()
        dbg_x1 = nc.dram_tensor("dbg_x1", [128, DT * Tc], BF16, kind="ExternalOutput").ap()
        dbg_wkv = nc.dram_tensor("dbg_wkv", [128, DT * Tc], BF16, kind="ExternalOutput").ap()
        dbg_aux = nc.dram_tensor("dbg_aux", [128, 352], F32, kind="ExternalOutput").ap()
        dbg_wkvr = nc.dram_tensor("dbg_wkvr", [128, DT * Tc], BF16, kind="ExternalOutput").ap()
        dbg_g = nc.dram_tensor("dbg_g", [128, DT * Tc], BF16, kind="ExternalOutput").ap()
        dbg_gath = nc.dram_tensor("dbg_gath", [128, NCORES * 48], F32, kind="ExternalOutput").ap()
    cc_in = nc.dram_tensor("cc_in", [128, 48], F32, kind="Internal").ap()
    cc_out = nc.dram_tensor("cc_out", [NCORES * 128, 48], F32, kind="Internal").ap()

    ctx = contextlib.ExitStack()
    E = ctx.enter_context
    with ctx:
        OF_AR1 = 16512
        OF_AR2 = OF_AR1 + 32800
        OF_ARZ = OF_AR2 + 32768
        OF_ARS = OF_ARZ + 65536
        OF_RING = OF_ARS + 32768
        OF_UG = OF_RING + 24576
        OF_VEC = OF_UG + 12288
        OF_AUX = OF_VEC + 2048
        OF_HM = OF_AUX + 1408
        OF_GATH = OF_HM + 800
        AT = nc.alloc_sbuf_tensor_at
        AR1 = AT("AR1", [128, DT * W1], BF16, offset=OF_AR1)
        AR2 = AT("AR2", [128, DT * Tc], BF16, offset=OF_AR2)
        ARZ = AT("ARZ", [128, 2 * DT * Tc], BF16, offset=OF_ARZ)
        ARS = AT("ARS", [128, 8192], F32, offset=OF_ARS)
        RING = AT("RING", [128, 6 * 2048], BF16, offset=OF_RING)
        B_ug = AT("B_ug", [128, 6 * 1024], BF16, offset=OF_UG)
        B_vec = AT("B_vec", [128, 512], F32, offset=OF_VEC)
        B_aux = AT("B_aux", [128, 352], F32, offset=OF_AUX)
        B_hm = AT("B_hm", [128, 385], BF16, offset=OF_HM)
        B_gath = AT("B_gath", [128, NCORES * 48], F32, offset=OF_GATH)
        B_slk = AT("B_slk", [128, 256], F32, offset=OF_GATH + 1536)
        VQ = AT("VQ", [128, 4 * 1024], BF16, offset=OF_AR2)
        VRIN = AT("VRIN", [128, DT * Tc], BF16, offset=OF_ARS)
        VEV = AT("VEV", [128, 1024], F32, offset=OF_UG)

        def xsl(tau, a, b):
            return AR1[:, tau * W1 + a: tau * W1 + b]

        def x1sl(a, b):
            return AR1[:, a:b]

        def gsl(tau, a, b):
            return AR1[:, tau * Tc + a: tau * Tc + b]

        kin = AR2
        wkvr = AR2
        kkv = ARZ
        KKOFF, VOFF = 0, DT * Tc

        def kksl(tau, a, b):
            return ARZ[:, KKOFF + tau * Tc + a: KKOFF + tau * Tc + b]

        def vsl(tau, a, b):
            return ARZ[:, VOFF + tau * Tc + a: VOFF + tau * Tc + b]

        def wkvsl(tau, a, b):
            return ARZ[:, tau * Tc + a: tau * Tc + b]

        def z3sl(a, b):
            return ARZ[:, a:b]

        SC_NT, SC_MX, SC_RS, SC_M, SC_R = 0, 1025, 3073, 5121, 6146
        SC_TB, SC_L, SC_E1 = 0, DT * C, 2 * DT * C
        SC_OT, SC_XF = 0, 2048

        def scr(c0, n):
            return ARS[:, c0:c0 + n]

        QE2, QA = 0, DT * C

        def e2sl(tau, a=0, b=None):
            b = C if b is None else b
            return VQ[:, QE2 + tau * C + a: QE2 + tau * C + b]

        def asl(tau, a=0, b=None):
            b = C if b is None else b
            return VQ[:, QA + tau * C + a: QA + tau * C + b]

        def vc(c0, n=16):
            return B_vec[:, c0:c0 + n]

        def vcol(c0):
            return B_vec[:, c0:c0 + 1]

        def acol(c0):
            return B_aux[:, c0:c0 + 1]

        B_mrow = ARS[0:1, 1025:1025 + W1]
        B_qrow = ARS[0:1, 1025 + W1:1025 + 2 * W1]
        ones_row = B_vec[0:1, VC_ONES:VC_ONES + 128]
        ones_col = B_hm[:, HM_ONE:HM_ONE + 1]

        ps = [E(nc.psum_tensor(f"ps{i}", [128, 512], F32)) for i in range(4)]
        psF = [E(nc.psum_tensor(f"psF{i}", [128, 512], F32)) for i in range(2)]
        psS = E(nc.psum_tensor("psS", [1, 512], F32))
        psQ = E(nc.psum_tensor("psQ", [1, 512], F32))

        s_xld = E(nc.semaphore("s_xld"))
        s_wsl = [E(nc.semaphore(f"s_wsl{i}")) for i in range(6)]
        s_ld = E(nc.semaphore("s_ld"))
        s_rsp = [E(nc.semaphore(f"s_rsp{i}")) for i in range(2)]
        s_sh = E(nc.semaphore("s_sh"))
        s_spe1 = E(nc.semaphore("s_spe1"))
        s_spa = E(nc.semaphore("s_spa"))
        s_spe2 = E(nc.semaphore("s_spe2"))
        s_rl = E(nc.semaphore("s_rl"))
        s_rlb = E(nc.semaphore("s_rlb"))
        s_put = E(nc.semaphore("s_put"))
        s_gt = E(nc.semaphore("s_gt"))
        s_ri = E(nc.semaphore("s_ri"))
        s_zw = E(nc.semaphore("s_zw"))
        s_xf = [E(nc.semaphore(f"s_xf{i}")) for i in range(2)]
        s_st = [E(nc.semaphore(f"s_st{i}")) for i in range(2)]
        s_za = [E(nc.semaphore(f"s_za{i}")) for i in range(2)]
        s_cba = E(nc.semaphore("s_cba"))
        s_cbb = E(nc.semaphore("s_cbb"))
        s_cbc = E(nc.semaphore("s_cbc"))
        s_cc = E(nc.semaphore("s_cc"))
        s_pe = E(nc.semaphore("s_pe"))
        s_ev = E(nc.semaphore("s_ev"))
        s_vp = E(nc.semaphore("s_vp"))
        s_sc = E(nc.semaphore("s_sc"))
        s_gp = E(nc.semaphore("s_gp"))

        plan = {"sync": [], "gpsimd": [], "vector": [], "scalar": [], "tensor": []}
        cnt = {}

        def op(eng, action, waits=(), inc=None, amt=1):
            ev = None
            if inc is not None:
                cnt[inc] = cnt.get(inc, 0) + (amt if amt else 16)
                ev = (inc, cnt[inc])
            plan[eng].append((tuple(w for w in waits if w is not None), action, inc, amt))
            return ev

        def last(sem):
            return (sem, cnt.get(sem, 0))

        def vslack(waits, n=2):
            ev = None
            for i in range(n):
                ev = op("vector", lambda v: v.tensor_scalar_mul(
                    B_slk[:, 0:256], B_slk[:, 0:256], 1.0),
                    waits=(list(waits) if i == 0 else []), inc=s_vp)
            return ev

        xin_r = xin.rearrange("(tau p) t -> p tau t", p=128)
        ev_xl = []
        for i in range(4):
            ev_xl.append(op("gpsimd", lambda g, i=i: g.dma_start(
                out=AR1[:, i * 4 * W1:(i + 1) * 4 * W1].rearrange("p (a t) -> p a t", t=W1),
                in_=xin_r[:, i * 4:(i + 1) * 4, :]), inc=s_xld, amt=16))

        NSLOT = 6
        wstate = {"cursor": 0, "used": 0, "grab": 0}
        wblocks = []
        wreleases = []
        wdefer = {}

        def wload(src, nsl):
            cur = wstate["cursor"]
            assert cur + nsl <= NSLOT, "ring wrap-skip"
            waits = []
            need = wstate["used"] + nsl - NSLOT
            if need > 0:
                marker = ("DEFER_REL", len(wdefer))
                wdefer[marker] = need
                waits.append(marker)
            wstate["used"] += nsl
            wstate["cursor"] = (cur + nsl) % NSLOT
            ev = op("gpsimd", lambda g, cur=cur, nsl=nsl, src=src: g.dma_start(
                out=RING[:, cur * 2048:(cur + nsl) * 2048], in_=src),
                waits=waits, inc=s_wsl[cur], amt=16)
            wblocks.append((cur, nsl, ev))
            return ev

        def wgrab():
            bi = wstate["grab"]
            wstate["grab"] += 1
            sl, nsl, ev = wblocks[bi]
            return RING[:, sl * 2048:(sl + nsl) * 2048], ev, nsl

        def wrelease(nsl, eng=None, waits=()):
            prev = wreleases[-1][0] if wreleases else 0
            wreleases.append((prev + nsl, last(s_pe)))

        for t_ in range(DT):
            wload(wk[t_], 1)
        for t_ in range(DT):
            wload(wv[t_], 1)

        op("sync", lambda s: s.dma_start(out=B_vec[:], in_=vecs), inc=s_ld, amt=16)
        op("sync", lambda s: s.dma_start(out=B_hm[:], in_=hmat), inc=s_ld, amt=16)
        ev_vec = (s_ld, 32)
        ev_hm = (s_ld, 32)

        op("vector", lambda v: v.memset(B_slk[:], 0.0))
        op("vector", lambda v: v.tensor_add(B_aux[:, AX_BKU:AX_BKU + 16],
                                            vc(VC_U), vc(VC_BK)), waits=[ev_vec])
        op("vector", lambda v: v.tensor_scalar_mul(B_aux[:, AX_WDP64:AX_WDP64 + 16],
                                                   vc(VC_WDP), 64.0))
        ev_u1 = op("vector", lambda v: v.tensor_mul(B_aux[:, AX_UPS:AX_UPS + 64],
                                                    vc(VC_WSU, 64), vc(VC_SUDP, 64)),
                   inc=s_vp)
        ev_u2 = vslack([ev_u1])
        op("vector", lambda v: v.tensor_scalar_mul(B_aux[:, AX_UPS:AX_UPS + 64],
                                                   B_aux[:, AX_UPS:AX_UPS + 64], 64.0),
           waits=[ev_u2])
        op("vector", lambda v: v.tensor_scalar_mul(B_aux[:, AX_GSC64:AX_GSC64 + 64],
                                                   vc(VC_GSC, 64), 64.0))
        op("vector", lambda v: v.memset(B_aux[:, AX_PDEC:AX_PDEC + 16], 0.0))
        op("vector", lambda v: v.memset(B_aux[:, AX_ACAR2:AX_ACAR2 + 32], 0.0))
        ev_aux0 = op("vector", lambda v: v.memset(B_aux[:, AX_A0:AX_A0 + 32], 0.0),
                     inc=s_vp)

        chunks0 = [(i * 512, 512) for i in range(Tc // 512)] + [(Tc, 1)]
        ev_sqmm = {}
        ev_rows = []
        for ci, (c0, n) in enumerate(chunks0):
            for tau in range(DT):
                waits = []
                if ci == 0:
                    waits.append((s_xld, 64))
                    if tau == 0:
                        waits.append(ev_hm)
                if ci > 0 and tau == 0:
                    waits.append(ev_rows[ci - 1])
                op("tensor", lambda t, tau=tau, c0=c0, n=n: t.matmul(
                    psS[0:1, 0:n], ones_col, xsl(tau, c0, c0 + n),
                    start=(tau == 0), stop=(tau == DT - 1)),
                   waits=waits, inc=(s_pe if tau == DT - 1 else None))
            ev_ssum = last(s_pe)
            for tau in range(DT):
                sqv = B_ug[:, (tau % 2) * 1024:(tau % 2) * 1024 + n]
                waits = [(s_xld, 64)] if ci == 0 else []
                if (ci, tau - 2) in ev_sqmm:
                    waits.append(ev_sqmm[(ci, tau - 2)])
                elif ci > 0 and tau < 2:
                    waits.append(ev_sqmm[(ci - 1, DT - 2 + tau)])
                if ci > 0 and tau == 0:
                    waits.append(ev_rows[ci - 1])
                evq = op("scalar", lambda s, tau=tau, c0=c0, n=n, sqv=sqv: s.activation(
                    sqv, xsl(tau, c0, c0 + n), AF.Square), waits=waits, inc=s_sc)
                ev_sqmm[(ci, tau)] = op("tensor", lambda t, tau=tau, n=n, sqv=sqv: t.matmul(
                    psQ[0:1, 0:n], ones_col, sqv,
                    start=(tau == 0), stop=(tau == DT - 1)),
                    waits=[evq], inc=s_pe)
            ev_qsum = last(s_pe)
            op("vector", lambda v, c0=c0, n=n: v.tensor_copy(
                B_mrow[0:1, c0:c0 + n], psS[0:1, 0:n]), waits=[ev_ssum])
            ev_rows.append(op("vector", lambda v, c0=c0, n=n: v.tensor_copy(
                B_qrow[0:1, c0:c0 + n], psQ[0:1, 0:n]), waits=[ev_qsum], inc=s_vp))

        op("vector", lambda v: v.tensor_scalar_mul(B_mrow[0:1, :], B_mrow[0:1, :], 1.0 / D))
        op("vector", lambda v: v.tensor_scalar_mul(B_qrow[0:1, :], B_qrow[0:1, :], 1.0 / D))
        op("vector", lambda v: v.tensor_mul(scr(SC_NT, W1)[0:1, :], B_mrow[0:1, :], B_mrow[0:1, :]))
        ev_var = op("vector", lambda v: v.tensor_sub(B_qrow[0:1, :], B_qrow[0:1, :],
                                                     scr(SC_NT, W1)[0:1, :]), inc=s_vp)
        ev_eps = op("vector", lambda v: v.tensor_scalar_add(
            B_qrow[0:1, :], B_qrow[0:1, :], 1e-5), inc=s_vp)
        op("scalar", lambda s: s.activation(B_qrow[0:1, :], B_qrow[0:1, :], AF.Ln),
           waits=[ev_eps], inc=s_sc)
        ev_rstd = op("scalar", lambda s: s.activation(B_qrow[0:1, :], B_qrow[0:1, :],
                                                      AF.Exp, scale=-0.5), inc=s_sc)

        ev_bc_last = None
        ev_bcR = []
        for ci, (c0, n) in enumerate(chunks0):
            pf = psF[ci % 2]
            waits = [ev_rstd, ev_var]
            if ci >= 2:
                waits.append(ev_bcR[ci - 2])
            e1_ = op("tensor", lambda t, c0=c0, n=n, pf=pf: t.matmul(
                pf[:, 0:n], ones_row, B_mrow[0:1, c0:c0 + n], start=True, stop=True),
                waits=waits, inc=s_pe)
            ec1 = op("vector", lambda v, c0=c0, n=n, pf=pf: v.tensor_copy(
                scr(SC_M, W1)[:, c0:c0 + n], pf[:, 0:n]), waits=[e1_], inc=s_vp)
            e2_ = op("tensor", lambda t, c0=c0, n=n, pf=pf: t.matmul(
                pf[:, 0:n], ones_row, B_qrow[0:1, c0:c0 + n], start=True, stop=True),
                waits=[ec1], inc=s_pe)
            ev_bc_last = op("vector", lambda v, c0=c0, n=n, pf=pf: v.tensor_copy(
                scr(SC_R, W1)[:, c0:c0 + n], pf[:, 0:n]), waits=[e2_], inc=s_vp)
            ev_bcR.append(ev_bc_last)

        ev_xn = []
        for tau in range(DT):
            nt = scr(SC_NT, W1)
            op("vector", lambda v, tau=tau, nt=nt: v.tensor_sub(
                nt, xsl(tau, 0, W1), scr(SC_M, W1)),
               waits=[ev_bc_last] if tau == 0 else [])
            ev_xn.append(op("vector", lambda v, tau=tau, nt=nt: v.tensor_mul(
                xsl(tau, 0, W1), nt, scr(SC_R, W1)), inc=s_vp))

        s_dbg = E(nc.semaphore("s_dbg")) if DBG else None
        ev_dbg_xn = None
        if DBG:
            ev_dbg_xn = op("sync", lambda s: s.dma_start(out=dbg_xn, in_=AR1[:]),
                           waits=[ev_xn[-1]], inc=s_dbg, amt=16)

        def mix(tm_c0, dst_flat_off, tau, waits=()):
            op("vector", lambda v, tau=tau: v.tensor_sub(
                scr(SC_MX, Tc), xsl(tau, 1, W1), xsl(tau, 0, W1 - 1)), waits=list(waits))
            return op("vector", lambda v, tau=tau, tm_c0=tm_c0, dst_flat_off=dst_flat_off:
                      v.scalar_tensor_tensor(
                          AR2[:, dst_flat_off + tau * Tc: dst_flat_off + (tau + 1) * Tc],
                          scr(SC_MX, Tc), vcol(tm_c0 + tau), xsl(tau, 0, W1 - 1),
                          ALU.mult, ALU.add), inc=s_vp)

        for tau in range(DT):
            mix(VC_TMK, 0, tau)
        ev_kmix = last(s_vp)

        def emit_proj(dst_kind, bias_base, rhs_off, first_waits, dst_fn):
            evs = []
            for e in range(DT):
                wview, wevt, nsl = wgrab()
                for th in range(TH):
                    gidx = e * TH + th
                    pbank = ps[gidx % 4]
                    for dtl in range(DT):
                        waits = []
                        if dtl == 0:
                            waits.append(wevt)
                            waits.append((s_ev, cnt.get(s_ev, 0) - 3))
                            if gidx == 0:
                                waits += list(first_waits)
                        op("tensor", lambda t, th=th, dtl=dtl, pbank=pbank, wview=wview,
                           rhs_off=rhs_off: t.matmul(
                               pbank[:, 0:512], wview[:, dtl * 128:(dtl + 1) * 128],
                               AR2[:, rhs_off + dtl * Tc + th * 512:
                                   rhs_off + dtl * Tc + th * 512 + 512]
                               if dst_kind != "r" else
                               VRIN[:, dtl * Tc + th * 512: dtl * Tc + th * 512 + 512],
                               start=(dtl == 0), stop=(dtl == DT - 1)),
                           waits=waits, inc=(s_pe if dtl == DT - 1 else None))
                    ev_grp = last(s_pe)
                    dst_ap, extra = dst_fn(e, th)
                    bias = acol(AX_BKU + e) if bias_base == "bku" else vcol(bias_base + e)
                    evs.append(op("scalar", lambda s, dst_ap=dst_ap, pbank=pbank, bias=bias:
                                  s.activation(dst_ap, pbank[:, 0:512], AF.Identity,
                                               bias=bias),
                                  waits=[ev_grp] + extra, inc=s_ev))
                wrelease(nsl)
            return evs

        ev_dbg_kin = None
        if DBG:
            ev_dbg_kin = op("sync", lambda s: s.dma_start(out=dbg_kin, in_=AR2[:]),
                            waits=[ev_kmix], inc=s_dbg, amt=16)
        ev_kk = emit_proj("kk", "bku", 0, [ev_kmix, ev_aux0] + ([ev_dbg_kin] if DBG else []),
                          lambda e, th: (kksl(e, th * 512, th * 512 + 512), []))

        if DBG:
            op("sync", lambda s: s.dma_start(out=dbg_kk, in_=ARZ[:, 0:DT * Tc]),
               waits=[ev_kk[-1]], inc=s_dbg, amt=16)
        for tau in range(DT):
            mix(VC_TMV, 0, tau, waits=[ev_kk[-1]] if tau == 0 else [])
        ev_vmix = last(s_vp)
        ev_vv = emit_proj("v", VC_BV, 0, [ev_vmix],
                          lambda e, th: (vsl(e, th * 512, th * 512 + 512), []))

        if DBG:
            op("sync", lambda s: s.dma_start(out=dbg_v, in_=ARZ[:, DT * Tc:2 * DT * Tc]),
               waits=[ev_vv[-1]], inc=s_dbg, amt=16)
        ev_rsp = []
        ev_rsl = []
        for tau in range(DT):
            rs = scr(SC_RS + (tau % 2) * 1024, Tc)
            op("vector", lambda v, tau=tau: v.tensor_sub(
                scr(SC_MX, Tc), xsl(tau, 1, W1), xsl(tau, 0, W1 - 1)))
            waits = [ev_rsl[tau - 2]] if tau >= 2 else []
            evm = op("vector", lambda v, tau=tau, rs=rs: v.scalar_tensor_tensor(
                rs, scr(SC_MX, Tc), vcol(VC_TMR + tau), xsl(tau, 0, W1 - 1),
                ALU.mult, ALU.add), waits=waits, inc=s_vp)
            stg = B_ug[:, (tau % 2) * 1024:(tau % 2) * 1024 + Tc]
            waits = [evm]
            if tau == 0:
                waits.append(ev_sqmm[(len(chunks0) - 1, DT - 1)])
            if tau >= 2:
                waits.append(ev_rsp[tau - 2])
            evsl = op("scalar", lambda s, stg=stg, rs=rs: s.activation(
                stg, rs, AF.Silu), waits=waits, inc=s_sc)
            ev_rsl.append(evsl)
            ev_rsp.append(op("sync", lambda s, tau=tau, stg=stg: s.dma_start(
                out=rinsp[:, tau * Tc:(tau + 1) * Tc], in_=stg),
                waits=[evsl], inc=s_rsp[tau % 2], amt=16))

        ev_xt = []
        for tau in range(DT):
            ev_xt.append(op("vector", lambda v, tau=tau: v.tensor_scalar_mul(
                xsl(tau, 1, W1), xsl(tau, 1, W1), vcol(VC_SUG + tau)),
                waits=[ev_dbg_xn] if (DBG and tau == 0) else [], inc=s_vp))

        ev_f1 = []
        for tau in range(DT):
            for th in range(TH):
                j = tau * TH + th
                pf = psF[j % 2]
                waits = [ev_xt[tau]]
                if j >= 2:
                    waits.append((s_ev, cnt.get(s_ev, 0) - 1))
                evm = op("tensor", lambda t, tau=tau, th=th, pf=pf: t.matmul(
                    pf[:, 0:512], B_hm[:, HM_H128:HM_H128 + 128],
                    xsl(tau, 1 + th * 512, 1 + th * 512 + 512),
                    start=True, stop=True), waits=waits, inc=s_pe)
                ev_f1.append(op("scalar", lambda s, tau=tau, th=th, pf=pf: s.activation(
                    xsl(tau, 1 + th * 512, 1 + th * 512 + 512), pf[:, 0:512],
                    AF.Identity), waits=[evm], inc=s_ev))

        ev_sho = []
        for i in range(4):
            ev_sho.append(op("sync", lambda s, i=i: s.dma_start(
                out=yb2k[:, i * 4:(i + 1) * 4, :],
                in_=AR1[:].rearrange("p (tau t) -> p tau t", t=W1)[:, i * 4:(i + 1) * 4, 1:W1]),
                waits=[ev_f1[-1]] if i == 0 else [], inc=s_sh, amt=16))
        yb2k_sh = yb2k.rearrange("(h l) tau t -> h tau l t", h=8, l=16)
        ev_shi = []
        for h in range(8):
            ev_shi.append(op("sync", lambda s, h=h: s.dma_start(
                out=AR1[h * 16:(h + 1) * 16, 0:16 * Tc].rearrange("p (l t) -> p l t", t=Tc),
                in_=yb2k_sh[h]),
                waits=[(s_sh, 64)] if h == 0 else [], inc=s_sh, amt=16))

        ev_x1 = []
        for j in range(DT * Tc // 512):
            pf = psF[j % 2]
            waits = [(s_sh, 192)] if j == 0 else []
            if j >= 2:
                waits.append((s_ev, cnt.get(s_ev, 0) - 1))
            evm = op("tensor", lambda t, j=j, pf=pf: t.matmul(
                pf[:, 0:512], B_hm[:, HM_I8H16:HM_I8H16 + 128],
                x1sl(j * 512, (j + 1) * 512), start=True, stop=True),
                waits=waits, inc=s_pe)
            ev_x1.append(op("scalar", lambda s, j=j, pf=pf: s.activation(
                x1sl(j * 512, (j + 1) * 512), pf[:, 0:512], AF.Identity,
                scale=S2048), waits=[evm], inc=s_ev))
        ev_x1_done = ev_x1[-1]
        if DBG:
            ev_x1_done = op("sync", lambda s: s.dma_start(out=dbg_x1, in_=AR1[:, 0:DT * Tc]),
                            waits=[ev_x1[-1]], inc=s_dbg, amt=16)

        P2_TB, P2_L = 0, Tc
        P2_E1 = [2 * Tc, 3 * Tc]
        P2_AO = [4 * Tc, 5 * Tc]
        P2_BO = [6 * Tc, 7 * Tc]
        Q_E2 = [0, Tc]
        Q_A = [2 * Tc, 3 * Tc]
        ev_spe1, ev_spa, ev_spe2 = {}, {}, {}
        ev_scan = {}
        for tau in range(DT):
            par = tau % 2
            waits = [ev_kk[(tau + 1) * TH - 1], ev_vv[(tau + 1) * TH - 1]]
            ev_tb = op("vector", lambda v, tau=tau: v.tensor_scalar(
                scr(P2_TB, Tc), kksl(tau, 0, Tc), vcol(VC_W + tau), None, ALU.max),
                waits=waits)
            ev_l = op("vector", lambda v, tau=tau: v.tensor_scalar(
                scr(P2_L, Tc), scr(P2_TB, Tc), -1.0, vcol(VC_W + tau),
                ALU.mult, ALU.subtract), inc=s_vp)
            op("vector", lambda v, tau=tau: v.reduce_sum(
                acol(AX_RTMP + tau), scr(P2_L, Tc), mybir.AxisListType.X))
            waits = [ev_l]
            if tau >= 2:
                waits.append(ev_spe1[tau - 2])
            ev_e1 = op("scalar", lambda s, par=par: s.activation(
                scr(P2_E1[par], Tc), scr(P2_L, Tc), AF.Exp), waits=waits, inc=s_sc)
            ev_sub = op("vector", lambda v, tau=tau: v.tensor_sub(
                scr(P2_TB, Tc), kksl(tau, 0, Tc), scr(P2_TB, Tc)), inc=s_vp)
            waits = [ev_sub]
            if tau >= 2:
                waits.append(ev_spe2[tau - 2])
            ev_e2 = op("scalar", lambda s, par=par: s.activation(
                VQ[:, Q_E2[par]:Q_E2[par] + Tc], scr(P2_TB, Tc), AF.Exp),
                waits=waits, inc=s_sc)
            waits = [ev_e2]
            if tau >= 2:
                waits.append(ev_spa[tau - 2])
            ev_A = op("vector", lambda v, tau=tau, par=par: v.tensor_mul(
                VQ[:, Q_A[par]:Q_A[par] + Tc], VQ[:, Q_E2[par]:Q_E2[par] + Tc],
                vsl(tau, 0, Tc)), waits=waits, inc=s_vp)
            op("vector", lambda v, par=par: v.tensor_tensor_scan(
                scr(P2_AO[par], Tc), scr(P2_E1[par], Tc),
                VQ[:, Q_A[par]:Q_A[par] + Tc], 0.0, ALU.mult, ALU.add))
            ev_sb = op("vector", lambda v, par=par: v.tensor_tensor_scan(
                scr(P2_BO[par], Tc), scr(P2_E1[par], Tc),
                VQ[:, Q_E2[par]:Q_E2[par] + Tc], 0.0, ALU.mult, ALU.add),
                inc=s_vp)
            ev_scan[tau] = ev_sb
            if tau >= 1:
                op("vector", lambda v, tau=tau, par=par: v.tensor_copy(
                    acol(AX_ACAR + tau - 1),
                    scr(P2_AO[1 - par], Tc)[:, Tc - 1:Tc]))
                op("vector", lambda v, tau=tau, par=par: v.tensor_copy(
                    acol(AX_BCAR + tau - 1),
                    scr(P2_BO[1 - par], Tc)[:, Tc - 1:Tc]))
            ev_spe1[tau] = op("sync", lambda s, tau=tau, par=par: s.dma_start(
                out=e1sp[tau], in_=scr(P2_E1[par], Tc)), waits=[ev_e1],
                inc=s_spe1, amt=16)
            ev_spe2[tau] = op("sync", lambda s, tau=tau, par=par: s.dma_start(
                out=e2sp[tau], in_=VQ[:, Q_E2[par]:Q_E2[par] + Tc]),
                waits=[ev_e2], inc=s_spe2, amt=16)
            ev_spa[tau] = op("sync", lambda s, tau=tau, par=par: s.dma_start(
                out=asp[tau], in_=VQ[:, Q_A[par]:Q_A[par] + Tc]),
                waits=[ev_scan[tau]], inc=s_spa, amt=16)

        ev_psum = op("vector", lambda v: v.tensor_add(
            B_aux[:, AX_PDEC:AX_PDEC + 16], B_aux[:, AX_PDEC:AX_PDEC + 16],
            B_aux[:, AX_RTMP:AX_RTMP + 16]), inc=s_vp)
        ev_pexp = op("scalar", lambda s: s.activation(
            B_aux[:, AX_PDEC:AX_PDEC + 16], B_aux[:, AX_PDEC:AX_PDEC + 16],
            AF.Exp), waits=[ev_psum], inc=s_sc)
        op("vector", lambda v: v.tensor_copy(
            acol(AX_ACAR + DT - 1), scr(P2_AO[(DT - 1) % 2], Tc)[:, Tc - 1:Tc]),
            waits=[ev_pexp])
        ev_lc15 = op("vector", lambda v: v.tensor_copy(
            acol(AX_BCAR + DT - 1), scr(P2_BO[(DT - 1) % 2], Tc)[:, Tc - 1:Tc]),
            inc=s_vp)

        ev_lc15s = vslack([ev_lc15])
        ev_put = op("sync", lambda s: s.dma_start(
            out=cc_in, in_=B_aux[:, AX_PDEC:AX_PDEC + 48]),
            waits=[ev_lc15s, ev_pexp], inc=s_put, amt=16)
        ev_gth = op("sync", lambda s: s.dma_start(
            out=B_gath[:].rearrange("p (j f) -> p j f", f=48),
            in_=cc_out.rearrange("(j p) f -> p j f", p=128)),
            waits=[(s_cc, 1)], inc=s_gt, amt=16)

        ev_zsp = {}
        ev_zm = {}
        ev_ug_evac = {}

        def zv_tmp(tfx, i):
            par = tfx % 2
            return B_ug[:, (par * 3 + i) * Tc:(par * 3 + i) * Tc + Tc]

        def emit_zm(tf):
            for th in range(TH):
                waits = [("UGE", tf, 0, th), ("UGE", tf, 1, th)]
                if tf >= 2 and th == 0:
                    waits.append(("ZSPW", tf - 2))
                ev_zm[(tf, th)] = op("vector", lambda v, th=th, tf=tf: v.tensor_mul(
                    zv_tmp(tf, 2)[:, th * 512:th * 512 + 512],
                    zv_tmp(tf, 1)[:, th * 512:th * 512 + 512],
                    zv_tmp(tf, 0)[:, th * 512:th * 512 + 512]),
                    waits=waits, inc=s_vp)

        ZM_PRE = 4
        for _tf in range(ZM_PRE):
            emit_zm(_tf)

        prev_cb = [ev_gth]
        def cop(action):
            ev = op("vector", action, waits=list(prev_cb), inc=s_vp)
            prev_cb[:] = [ev]
        for j in range(NCORES - 1):
            Ft = B_aux[:, AX_FT:AX_FT + 16]
            Tt = B_aux[:, AX_TT:AX_TT + 16]
            Tb = B_aux[:, AX_TT + 16:AX_TT + 32]
            cop(lambda v, j=j, Ft=Ft: v.tensor_scalar(
                Ft, B_gath[:, j * 48:j * 48 + 16], vcol(VC_PM + j), vcol(VC_PMI + j),
                ALU.mult, ALU.add))
            cop(lambda v, j=j, Tt=Tt: v.tensor_scalar(
                Tt, B_gath[:, j * 48 + 16:j * 48 + 32], vcol(VC_PM + j), None, ALU.mult))
            cop(lambda v, j=j, Tb=Tb: v.tensor_scalar(
                Tb, B_gath[:, j * 48 + 32:j * 48 + 48], vcol(VC_PM + j), None, ALU.mult))
            cop(lambda v, Ft=Ft: v.tensor_mul(
                B_aux[:, AX_A0:AX_A0 + 16], B_aux[:, AX_A0:AX_A0 + 16], Ft))
            cop(lambda v, Tt=Tt: v.tensor_add(
                B_aux[:, AX_A0:AX_A0 + 16], B_aux[:, AX_A0:AX_A0 + 16], Tt))
            cop(lambda v, Ft=Ft: v.tensor_mul(
                B_aux[:, AX_B0:AX_B0 + 16], B_aux[:, AX_B0:AX_B0 + 16], Ft))
            cop(lambda v, Tb=Tb: v.tensor_add(
                B_aux[:, AX_B0:AX_B0 + 16], B_aux[:, AX_B0:AX_B0 + 16], Tb))
        ev_comb_v = vslack([last(s_vp)])
        ev_comb = op("sync", lambda s: s.dma_start(
            out=B_aux[:, AX_A02:AX_A02 + 32], in_=B_aux[:, AX_A0:AX_A0 + 32]),
            waits=[ev_comb_v], inc=s_cbc, amt=16)

        if DBG:
            ev_dbg_cb = op("sync", lambda s: s.dma_start(out=dbg_aux, in_=B_aux[:]),
                           waits=[ev_comb], inc=s_dbg, amt=16)
            op("sync", lambda s: s.dma_start(out=dbg_gath, in_=B_gath[:]),
               inc=s_dbg, amt=16)

        ev_used = {}
        for tau in range(DT):
            par = tau % 2
            waits = []
            if tau < 2:
                waits += [ev_lc15, (s_spe1, 16 * DT), (s_spa, 16 * DT),
                          (s_spe2, 16 * DT)]
            else:
                waits.append(ev_used[tau - 2])
            srl = s_rl if par == 0 else s_rlb
            ev_r1 = op("sync", lambda s, tau=tau, par=par: s.dma_start(
                out=scr(P2_E1[par], Tc), in_=e1sp[tau]), waits=waits,
                inc=srl, amt=16)
            ev_r2 = op("sync", lambda s, tau=tau, par=par: s.dma_start(
                out=VQ[:, Q_A[par]:Q_A[par] + Tc], in_=asp[tau]), inc=srl, amt=16)
            ev_r3 = op("sync", lambda s, tau=tau, par=par: s.dma_start(
                out=VQ[:, Q_E2[par]:Q_E2[par] + Tc], in_=e2sp[tau]), inc=srl, amt=16)
            waits = [ev_r3]
            if tau == 0:
                waits.append(ev_comb)
            op("vector", lambda v, tau=tau, par=par: v.tensor_tensor_scan(
                scr(P2_AO[par], Tc), scr(P2_E1[par], Tc),
                VQ[:, Q_A[par]:Q_A[par] + Tc], acol(AX_A02 + tau),
                ALU.mult, ALU.add), waits=waits)
            op("vector", lambda v, tau=tau, par=par: v.tensor_tensor_scan(
                scr(P2_BO[par], Tc), scr(P2_E1[par], Tc),
                VQ[:, Q_E2[par]:Q_E2[par] + Tc], acol(AX_A02 + 16 + tau),
                ALU.mult, ALU.add), waits=[ev_r3])
            op("vector", lambda v, par=par: v.reciprocal(
                scr(P2_BO[par], Tc), scr(P2_BO[par], Tc)))
            ev_used[tau] = op("vector", lambda v, tau=tau, par=par: v.tensor_mul(
                wkvsl(tau, 0, Tc), scr(P2_AO[par], Tc), scr(P2_BO[par], Tc)),
                inc=s_vp)
            if ZM_PRE + tau < FT:
                emit_zm(ZM_PRE + tau)
        for _tf in range(ZM_PRE + DT, FT):
            emit_zm(_tf)
        ev_wkv_done = ev_used[DT - 1]
        if DBG:
            ev_wkv_done = op("sync", lambda s: s.dma_start(out=dbg_wkv, in_=ARZ[:, 0:DT * Tc]),
                             waits=[ev_used[DT - 1]], inc=s_dbg, amt=16)

        UPLOOK = 3
        ev_upld = {}
        for tf in range(min(UPLOOK, FT)):
            ev_upld[(tf, 0)] = wload(wup[tf], 1)
            ev_upld[(tf, 1)] = wload(wgt[tf], 1)
        op("gpsimd", lambda g: g.collective_compute(
            "AllGather", ALU.bypass, replica_groups=[list(range(NCORES))],
            ins=[cc_in.opt()], outs=[cc_out.opt()]),
            waits=[ev_put], inc=s_cc, amt=0)

        ev_st1 = {}

        def zview(tfx):
            par = tfx % 2
            return B_ug[:, (par * 3 + 2) * Tc:(par * 3 + 2) * Tc + Tc]

        for tf in range(FT + 2):
            tfp = tf - 1
            if 0 <= tfp < FT:
                tz = zview(tfp)
                for th in range(TH):
                    pf = psF[th % 2]
                    waits = [ev_zm[(tfp, th)], (s_ev, cnt.get(s_ev, 0) - 1)]
                    evm = op("tensor", lambda t, th=th, pf=pf, tz=tz: t.matmul(
                        pf[:, 0:512], B_hm[:, HM_H128:HM_H128 + 128],
                        tz[:, th * 512:th * 512 + 512], start=True, stop=True),
                        waits=waits, inc=s_pe)
                    ev_st1[(tfp, th)] = op("scalar", lambda s, th=th, pf=pf, tz=tz:
                        s.activation(tz[:, th * 512:th * 512 + 512], pf[:, 0:512],
                                     AF.Identity), waits=[evm], inc=s_ev)
            tfs = tf - 2
            if 0 <= tfs < FT:
                tz = zview(tfs)
                ev_zsp[tfs] = op("scalar", lambda s, tfs=tfs, tz=tz: s.dma_start(
                    out=yb8k[:, tfs, :], in_=tz),
                    waits=[ev_st1[(tfs, TH - 1)]], inc=s_za[tfs % 2], amt=16)
            if tf >= FT:
                continue
            par = tf % 2
            tmpU = B_ug[:, (par * 3 + 0) * Tc:(par * 3 + 0) * Tc + Tc]
            tmpG = B_ug[:, (par * 3 + 1) * Tc:(par * 3 + 1) * Tc + Tc]
            tmpZ = zview(tf)
            for wi, (tmp, sc_c, func) in enumerate(
                    ((tmpU, AX_UPS, AF.Identity), (tmpG, AX_GSC64, AF.Silu))):
                wview, wevt, nsl = wgrab()
                for th in range(TH):
                    gidx = (tf * 2 + wi) * TH + th
                    pbank = ps[gidx % 4]
                    for l in range(16):
                        waits = []
                        if l == 0:
                            waits.append(wevt)
                            waits.append((s_ev, cnt.get(s_ev, 0) - 3))
                            if tf == 0 and wi == 0 and th == 0:
                                waits.append(ev_x1_done)
                        op("tensor", lambda t, l=l, th=th, pbank=pbank, wview=wview: t.matmul(
                            pbank[:, 0:512], wview[:, l * 128:(l + 1) * 128],
                            x1sl(l * Tc + th * 512, l * Tc + th * 512 + 512),
                            start=(l == 0), stop=(l == 15)),
                           waits=waits, inc=(s_pe if l == 15 else None))
                    ev_grp = last(s_pe)
                    waits = [ev_grp]
                    if tf >= 2 and th == 0 and wi == 0:
                        waits.append(ev_zm[(tf - 2, TH - 1)])
                    ev_ug_evac[(tf, wi, th)] = op("scalar",
                        lambda s, th=th, tmp=tmp, pbank=pbank, sc_c=sc_c, func=func, tf=tf:
                        s.activation(tmp[:, th * 512:th * 512 + 512], pbank[:, 0:512],
                                     func, scale=acol(sc_c + tf)),
                        waits=waits, inc=s_ev)
                wrelease(nsl)
            if tf + UPLOOK < FT:
                ev_upld[(tf + UPLOOK, 0)] = wload(wup[tf + UPLOOK], 1)
                ev_upld[(tf + UPLOOK, 1)] = wload(wgt[tf + UPLOOK], 1)

        for t_ in range(DT):
            wload(wr[t_], 1)
        ev_rld = []
        for i in range(4):
            waits = []
            if i == 0:
                waits = [ev_wkv_done, last(s_rsp[0]), last(s_rsp[1]),
                         last(s_rl), last(s_rlb)]
            ev_rld.append(op("sync", lambda s, i=i: s.dma_start(
                out=VRIN[:, i * 4 * Tc:(i + 1) * 4 * Tc],
                in_=rinsp[:, i * 4 * Tc:(i + 1) * 4 * Tc]),
                waits=waits, inc=s_ri, amt=16))

        ev_wkvr = []
        for e in range(DT):
            wview, wevt, nsl = wgrab()
            for th in range(TH):
                gidx = e * TH + th
                pbank = ps[gidx % 4]
                for dtl in range(DT):
                    waits = []
                    if dtl == 0:
                        waits.append(wevt)
                        waits.append((s_ev, cnt.get(s_ev, 0) - 3))
                        if gidx == 0:
                            waits.append((s_ri, 64))
                    op("tensor", lambda t, th=th, dtl=dtl, pbank=pbank, wview=wview: t.matmul(
                        pbank[:, 0:512], wview[:, dtl * 128:(dtl + 1) * 128],
                        VRIN[:, dtl * Tc + th * 512: dtl * Tc + th * 512 + 512],
                        start=(dtl == 0), stop=(dtl == DT - 1)),
                       waits=waits, inc=(s_pe if dtl == DT - 1 else None))
                ev_grp = last(s_pe)
                sl = VEV[:, (gidx % 2) * 512:(gidx % 2) * 512 + 512]
                waits = [ev_grp]
                if gidx == 0:
                    waits.append(last(s_za[0]))
                    waits.append(last(s_za[1]))
                    waits.append(ev_zm[(FT - 1, TH - 1)])
                    waits.append(ev_zm[(FT - 2, TH - 1)])
                if gidx >= 2:
                    waits.append(ev_wkvr[gidx - 2])
                eve = op("scalar", lambda s, sl=sl, pbank=pbank, e=e: s.activation(
                    sl, pbank[:, 0:512], AF.Identity, bias=vcol(VC_BR + e)),
                    waits=waits, inc=s_ev)
                waits = [eve]
                if gidx == 0:
                    waits.append(ev_wkv_done)
                ev_wkvr.append(op("vector", lambda v, e=e, th=th, sl=sl: v.tensor_mul(
                    wkvr[:, e * Tc + th * 512: e * Tc + th * 512 + 512], sl,
                    wkvsl(e, th * 512, th * 512 + 512)), waits=waits, inc=s_vp))
            wrelease(nsl)

        if DBG:
            op("sync", lambda s: s.dma_start(out=dbg_wkvr, in_=AR2[:]),
               waits=[ev_wkvr[-1]], inc=s_dbg, amt=16)
        for rep in range(max(2, TH)):
            for to in range(DT):
                wload(wdn[to * 2 + 0], 2)
                wload(wdn[to * 2 + 1], 2)
        for t_ in range(DT):
            wload(wo[t_], 1)

        yb8k_sh = yb8k.rearrange("(h l) tau t -> h tau l t", h=2, l=64)
        NW = max(2, TH)
        TW = Tc // NW
        ev_g = []
        for th2 in range(NW):
            ev_w = []
            for h in range(2):
                waits = []
                if h == 0:
                    waits.append(last(s_za[0]))
                    waits.append(last(s_za[1]))
                    if th2 == 0:
                        waits.append(ev_wkvr[-1])
                    else:
                        waits.append(last(s_pe))
                ev_w.append(op("sync", lambda s, h=h, th2=th2: s.dma_start(
                    out=ARZ[h * 64:(h + 1) * 64, 0:64 * TW]
                        .rearrange("p (l t) -> p l t", t=TW),
                    in_=yb8k_sh[h, :, :, th2 * TW:th2 * TW + TW]),
                    waits=waits, inc=s_zw, amt=16))
            ev_s2 = []
            for j in range(64 * TW // 512):
                pf = psF[j % 2]
                waits = [(s_zw, 32 * (th2 + 1))] if j == 0 else []
                if j >= 2:
                    waits.append((s_ev, cnt.get(s_ev, 0) - 1))
                evm = op("tensor", lambda t, j=j, pf=pf: t.matmul(
                    pf[:, 0:512], B_hm[:, HM_I2H64:HM_I2H64 + 128],
                    z3sl(j * 512, (j + 1) * 512), start=True, stop=True),
                    waits=waits, inc=s_pe)
                ev_s2.append(op("scalar", lambda s, j=j, pf=pf: s.activation(
                    z3sl(j * 512, (j + 1) * 512), pf[:, 0:512], AF.Identity,
                    scale=S8192), waits=[evm], inc=s_ev))
            for to in range(DT):
                wv0, wevt0, n0 = wgrab()
                wv1, wevt1, n1 = wgrab()
                pbank = ps[to % 4]
                for l in range(64):
                    wview = wv0 if l < 32 else wv1
                    loc = l if l < 32 else l - 32
                    waits = []
                    if l == 0:
                        waits += [wevt0, wevt1, ev_s2[-1],
                                  (s_ev, cnt.get(s_ev, 0) - 3)]
                    op("tensor", lambda t, l=l, loc=loc, pbank=pbank, wview=wview: t.matmul(
                        pbank[:, 0:TW], wview[:, loc * 128:(loc + 1) * 128],
                        z3sl(l * TW, (l + 1) * TW),
                        start=(l == 0), stop=(l == 63)),
                       waits=waits, inc=(s_pe if l == 63 else None))
                ev_grp = last(s_pe)
                ev_g.append(op("scalar", lambda s, to=to, th2=th2, pbank=pbank: s.activation(
                    gsl(to, th2 * TW, th2 * TW + TW), pbank[:, 0:TW], AF.Silu,
                    scale=acol(AX_WDP64 + to)), waits=[ev_grp], inc=s_ev))
                wrelease(n0 + n1)
        ev_g_done = ev_g[-1]

        if DBG:
            ev_g_done = op("sync", lambda s: s.dma_start(
                out=dbg_g, in_=AR1[:, 0:DT * Tc]), waits=[ev_g[-1]],
                inc=s_dbg, amt=16)
        ev_wkvr_s = vslack([ev_wkvr[-1]])
        xf_ev = {}
        mul_ev = {}
        comb_ev = {}
        st_ev = {}
        for e in range(DT):
            wview, wevt, nsl = wgrab()
            for th in range(TH):
                idx = e * TH + th
                xf_slot = scr(SC_XF, 2048)[:, (idx % 2) * 512:(idx % 2) * 512 + 512]
                waits = []
                if idx == 0:
                    waits.append(ev_wkv_done)
                    waits.append((s_ri, 64))
                    waits.append(ev_wkvr[-1])
                if idx >= 2:
                    waits.append(comb_ev[idx - 2])
                xf_ev[idx] = op("sync", lambda s, e=e, th=th, xf_slot=xf_slot: s.dma_start(
                    out=xf_slot, in_=xin_r[:, e, 1 + th * 512:1 + th * 512 + 512]),
                    waits=waits, inc=s_xf[idx % 2], amt=16)
                pbank = ps[idx % 4]
                for dtl in range(DT):
                    mwaits = []
                    if dtl == 0:
                        mwaits.append(wevt)
                        mwaits.append((s_ev, cnt.get(s_ev, 0) - 3))
                        if idx == 0:
                            mwaits.append(ev_wkvr_s)
                    op("tensor", lambda t, th=th, dtl=dtl, pbank=pbank, wview=wview: t.matmul(
                        pbank[:, 0:512], wview[:, dtl * 128:(dtl + 1) * 128],
                        wkvr[:, dtl * Tc + th * 512: dtl * Tc + th * 512 + 512],
                        start=(dtl == 0), stop=(dtl == DT - 1)),
                       waits=mwaits, inc=(s_pe if dtl == DT - 1 else None))
                ev_grp = last(s_pe)
                sl = VEV[:, (idx % 2) * 512:(idx % 2) * 512 + 512]
                waits = [ev_grp]
                if idx >= 2:
                    waits.append(mul_ev[idx - 2])
                eve = op("scalar", lambda s, sl=sl, pbank=pbank, e=e: s.activation(
                    sl, pbank[:, 0:512], AF.Identity, bias=vcol(VC_BO + e)),
                    waits=waits, inc=s_ev)
                ot = scr(SC_OT, 2048)[:, (idx % 2) * 512:(idx % 2) * 512 + 512]
                waits = [eve, xf_ev[idx]]
                if idx == 0:
                    waits.append(ev_g_done)
                if idx >= 3:
                    waits.append(st_ev[idx - 2])
                elif idx == 2:
                    waits.append(st_ev[0])
                mul_ev[idx] = op("vector", lambda v, sl=sl, ot=ot, e=e, th=th: v.tensor_mul(
                    ot, sl, gsl(e, th * 512, th * 512 + 512)), waits=waits, inc=s_vp)
                comb_ev[idx] = op("vector", lambda v, ot=ot, xf_slot=xf_slot: v.tensor_add(
                    ot, ot, xf_slot), inc=s_vp)
                if idx >= 1:
                    pidx = idx - 1
                    pot = scr(SC_OT, 2048)[:, (pidx % 2) * 512:(pidx % 2) * 512 + 512]
                    pe_, pth = pidx // TH, pidx % TH
                    st_ev[pidx] = op("sync", lambda s, pe_=pe_, pth=pth, pot=pot: s.dma_start(
                        out=out[:, pe_ * Tc + pth * 512: pe_ * Tc + pth * 512 + 512],
                        in_=pot), waits=[comb_ev[pidx]], inc=s_st[pidx % 2], amt=16)
            wrelease(nsl)

        lidx = DT * TH - 1
        ev_lsl = vslack([comb_ev[lidx]])
        st_ev[lidx] = op("sync", lambda s: s.dma_start(
            out=out[:, (DT - 1) * Tc + (TH - 1) * 512: DT * Tc], 
            in_=scr(SC_OT, 2048)[:, (lidx % 2) * 512:(lidx % 2) * 512 + 512]),
            waits=[ev_lsl], inc=s_st[lidx % 2], amt=16)
        op("sync", lambda s: s.nop(), waits=[last(s_st[0]), last(s_st[1])])
        op("gpsimd", lambda g: g.nop(), waits=[last(s_st[0]), last(s_st[1])])

        assert wstate["grab"] == len(wblocks), (wstate, len(wblocks))

        block = E(nc.Block())

        def resolve_defer(marker):
            kind = marker[0]
            if kind == "DEFER_REL":
                need = wdefer[marker]
                for cum, ev in wreleases:
                    if cum >= need:
                        return ev
                raise AssertionError("unresolvable ring release")
            if kind == "UGE":
                return ev_ug_evac[marker[1:]]
            if kind == "ZSPW":
                return ev_zsp.get(marker[1])
            raise AssertionError(marker)

        def make_runner(items):
            def runner(eng):
                for waits, action, inc_sem, amt in items:
                    for wt in waits:
                        if isinstance(wt[0], str):
                            wt = resolve_defer(wt)
                            if wt is None:
                                continue
                        (sem, val) = wt
                        if val > 0:
                            eng.wait_ge(sem, val)
                    ins = action(eng)
                    if inc_sem is not None:
                        if amt:
                            ins.then_inc(inc_sem, amt)
                        else:
                            ins.then_inc(inc_sem)
            return runner

        block.sync(make_runner(plan["sync"]))
        block.gpsimd(make_runner(plan["gpsimd"]))
        block.vector(make_runner(plan["vector"]))
        block.scalar(make_runner(plan["scalar"]))
        block.tensor(make_runner(plan["tensor"]))

    return nc



def _H(n):
    h = np.array([[1.0]], dtype=np.float32)
    while h.shape[0] < n:
        h = np.block([[h, h], [h, -h]])
    return h


def _pack_attn(W):
    WT = np.ascontiguousarray(np.asarray(W, np.float32).T)
    return np.ascontiguousarray(
        WT.reshape(16, 128, 16, 128).transpose(2, 1, 0, 3).reshape(16, 128, 16 * 128))


def _pack_upgate(W):
    W = np.asarray(W, np.float32)
    P2 = np.arange(128)
    l = np.arange(16)
    dmap = (P2[:, None] % 16) * 128 + (P2[:, None] // 16) * 16 + l[None, :]
    A = W.T[dmap, :]
    return np.ascontiguousarray(
        A.reshape(128, 16, 64, 128).transpose(2, 0, 1, 3).reshape(64, 128, 16 * 128))


def _pack_down(W):
    W = np.asarray(W, np.float32)
    P3 = np.arange(128)
    l = np.arange(64)
    fmap = (P3[:, None] % 64) * 128 + (P3[:, None] // 64) * 64 + l[None, :]
    B = W.T[fmap, :]
    Bb = B.reshape(128, 2, 32, 16, 128).transpose(3, 1, 0, 2, 4)
    return np.ascontiguousarray(Bb.reshape(16 * 2, 128, 32 * 128))


def _v2(vec):
    return np.ascontiguousarray(np.asarray(vec, np.float32).reshape(-1, 128).T)


def kernel(L0_attn_x, L0_attn_Wk, L0_attn_bk, L0_attn_Wv, L0_attn_bv,
           L0_attn_Wr, L0_attn_br, L0_attn_Wo, L0_attn_bo, L0_attn_tmk,
           L0_attn_tmv, L0_attn_tmr, L0_attn_u, L0_attn_w, L0_ffn_SU_ug,
           L0_ffn_SU_dp, L0_ffn_Wscale_ug, L0_ffn_Wscale_dp, L0_ffn_W_up,
           L0_ffn_W_gate, L0_ffn_W_down):
    T = L0_attn_x.shape[1]
    Tc = T // NCORES
    C = 128
    x = np.asarray(L0_attn_x, np.float32)[0]
    xT = np.ascontiguousarray(x.T)

    wk = _pack_attn(L0_attn_Wk)
    wvv = _pack_attn(L0_attn_Wv)
    wrr = _pack_attn(L0_attn_Wr)
    woo = _pack_attn(L0_attn_Wo)
    wupp = _pack_upgate(L0_ffn_W_up)
    wgtt = _pack_upgate(L0_ffn_W_gate)
    wdnn = _pack_down(L0_ffn_W_down)

    hm = np.zeros((128, 385), np.float32)
    hm[:, 0:128] = _H(128)
    hm[:, 128:256] = np.kron(np.eye(8, dtype=np.float32), _H(16))
    hm[:, 256:384] = np.kron(np.eye(2, dtype=np.float32), _H(64))
    hm[:, 384] = 1.0
    hm = hm.astype(ml_dtypes.bfloat16)

    in_maps = []
    for c in range(NCORES):
        t0 = c * Tc
        prev = np.zeros((D, 1), np.float32) if c == 0 else xT[:, t0 - 1:t0]
        xin = np.ascontiguousarray(np.concatenate([prev, xT[:, t0:t0 + Tc]], axis=1))
        vv = np.zeros((128, 512), np.float32)
        vv[:, VC_TMK:VC_TMK + 16] = _v2(L0_attn_tmk)
        vv[:, VC_TMV:VC_TMV + 16] = _v2(L0_attn_tmv)
        vv[:, VC_TMR:VC_TMR + 16] = _v2(L0_attn_tmr)
        vv[:, VC_W:VC_W + 16] = _v2(L0_attn_w)
        vv[:, VC_U:VC_U + 16] = _v2(L0_attn_u)
        vv[:, VC_BK:VC_BK + 16] = _v2(L0_attn_bk)
        vv[:, VC_BV:VC_BV + 16] = _v2(L0_attn_bv)
        vv[:, VC_BR:VC_BR + 16] = _v2(L0_attn_br)
        vv[:, VC_BO:VC_BO + 16] = _v2(L0_attn_bo)
        vv[:, VC_SUG:VC_SUG + 16] = _v2(L0_ffn_SU_ug)
        vv[:, VC_WDP:VC_WDP + 16] = _v2(L0_ffn_Wscale_dp)
        vv[:, VC_WSU:VC_WSU + 64] = _v2(np.asarray(L0_ffn_Wscale_ug)[:FF])
        vv[:, VC_SUDP:VC_SUDP + 64] = _v2(L0_ffn_SU_dp)
        vv[:, VC_GSC:VC_GSC + 64] = _v2(np.asarray(L0_ffn_Wscale_ug)[FF:])
        vv[:, VC_PM:VC_PM + 8] = (np.arange(8)[None, :] < c).astype(np.float32)
        vv[:, VC_PMI:VC_PMI + 8] = 1.0 - vv[:, VC_PM:VC_PM + 8]
        vv[:, VC_ONES:VC_ONES + 128] = 1.0
        in_maps.append({
            "xin": xin, "wk": wk, "wv": wvv, "wr": wrr, "wo": woo,
            "wup": wupp, "wgt": wgtt, "wdn": wdnn, "vecs": vv, "hmat": hm,
        })

    nc = build_program(Tc, C)
    res = run_bass_kernel_spmd(nc, in_maps, core_ids=list(range(NCORES)))

    outs = []
    for c in range(NCORES):
        oc = np.asarray(res.results[c]["out"])
        outs.append(oc.reshape(128, DT, Tc).transpose(1, 0, 2).reshape(D, Tc).T)
    return np.ascontiguousarray(np.concatenate(outs, axis=0)[None]).astype(np.float32)
